# version 01
# baseline (speedup 1.0000x reference)
import contextlib
import numpy as np
import ml_dtypes
import concourse.bass as bass
import concourse.mybir as mybir
from concourse.bass_utils import run_bass_kernel_spmd
from concourse.bass import IndirectOffsetOnAxis

F32 = mybir.dt.float32
BF16 = mybir.dt.bfloat16
I32 = mybir.dt.int32
AF = mybir.ActivationFunctionType
ALU = mybir.AluOpType
BF = ml_dtypes.bfloat16

NCORES = 8
B = 2
T = 8192
D = 2048
NTOK = B * T
TC = NTOK // NCORES
D_INNER = 4096
D_STATE = 128
NHEADS = 64
HEADDIM = 64
CONV_DIM = D_INNER + 2 * D_STATE
D_IN_PROJ = 2 * D_INNER + 2 * D_STATE + NHEADS
CHUNK = 128
LN_EPS = 1e-5

_PASSTHRU = {"dma_start", "dma_start_transpose", "indirect_dma_start",
             "collective_compute", "wait_ge", "nop", "engine_nop"}


class Eng:

    def __init__(self, ctx, nc, eng, name):
        self._e = eng
        self.h = ctx.enter_context(nc.semaphore(name))
        self.n = 0

    def wait(self, other, n=None):
        self._e.wait_ge(other.h, other.n if n is None else n)

    def raw(self):
        return self._e

    def __getattr__(self, name):
        f = getattr(self._e, name)
        if name in _PASSTHRU or name.startswith("_"):
            return f

        def wrap(*a, **k):
            if self.n:
                self._e.wait_ge(self.h, self.n)
            inst = f(*a, **k)
            self.n += 1
            inst.then_inc(self.h, 1)
            return inst
        return wrap


class Sem:

    def __init__(self, ctx, nc, name):
        self.h = ctx.enter_context(nc.semaphore(name))
        self.n = 0


class Dma:

    def __init__(self, ctx, nc, eng, name):
        self._e = eng
        self._ctx = ctx
        self._nc = nc
        self.q = Sem(ctx, nc, name)

    def wait(self, other, n=None):
        self._e.wait_ge(other.h, other.n if n is None else n)

    def start(self, out, in_, sem=None, **kw):
        sem = sem or self.q
        inst = self._e.dma_start(out=out, in_=in_, **kw)
        sem.n += 16
        inst.then_inc(sem.h, 16)
        return inst

    def indirect(self, out, out_offset, in_, in_offset, sem=None, **kw):
        sem = sem or self.q
        inst = self._e.indirect_dma_start(out=out, out_offset=out_offset,
                                          in_=in_, in_offset=in_offset, **kw)
        sem.n += 16
        inst.then_inc(sem.h, 16)
        return inst

    def transpose(self, out, in_, sem=None):
        sem = sem or self.q
        inst = self._e.dma_start(out=out, in_=in_, transpose=True)
        sem.n += 16
        inst.then_inc(sem.h, 16)
        return inst

    def collective(self, kind, op, replica_groups, ins, outs, sem=None):
        sem = sem or self.q
        inst = self._e.collective_compute(kind, op, replica_groups=replica_groups,
                                          ins=ins, outs=outs)
        sem.n += 1
        inst.then_inc(sem.h, 1)
        return inst


class Pe:

    def __init__(self, ctx, nc, name="pe"):
        self._e = nc.tensor
        self.h = ctx.enter_context(nc.semaphore(name))
        self.n = 0

    def wait(self, other, n=None):
        self._e.wait_ge(other.h, other.n if n is None else n)

    def matmul(self, *a, **k):
        return self._e.matmul(*a, **k)

    def transpose(self, *a, **k):
        return self._e.transpose(*a, **k)

    def done(self, inst):
        self.n += 1
        inst.then_inc(self.h, 1)
        return self.n


def _bf(x):
    return np.ascontiguousarray(np.asarray(x).astype(BF))


def _f32(x):
    return np.ascontiguousarray(np.asarray(x).astype(np.float32))



def build_mamba_front():
    nc = bass.Bass("TRN2", target_bir_lowering=False)
    TE = TC + 3
    xt = nc.dram_tensor("xt", [D, TE], F32, kind="ExternalInput").ap()
    wT = nc.dram_tensor("wT", [D, D_IN_PROJ], F32, kind="ExternalInput").ap()
    cw = nc.dram_tensor("cw", [128, 34 * 4], F32, kind="ExternalInput").ap()
    cb = nc.dram_tensor("cb", [128, 34], F32, kind="ExternalInput").ap()
    dtb = nc.dram_tensor("dtb", [NHEADS, 1], F32, kind="ExternalInput").ap()
    alog = nc.dram_tensor("alog", [NHEADS, 1], F32, kind="ExternalInput").ap()
    ones_bf = nc.dram_tensor("ones_bf", [128, 128], BF16, kind="ExternalInput").ap()

    zs_o = nc.dram_tensor("zs", [D_INNER, TC], BF16, kind="ExternalOutput").ap()
    xc_o = nc.dram_tensor("xc", [D_INNER, TC], BF16, kind="ExternalOutput").ap()
    bf_o = nc.dram_tensor("Bf", [D_STATE, TC], BF16, kind="ExternalOutput").ap()
    cf_o = nc.dram_tensor("Cf", [D_STATE, TC], BF16, kind="ExternalOutput").ap()
    dtv_o = nc.dram_tensor("dtv", [NHEADS, TC], F32, kind="ExternalOutput").ap()
    cuma_o = nc.dram_tensor("cumA", [NHEADS, TC], F32, kind="ExternalOutput").ap()
    xn_o = nc.dram_tensor("xn", [D, TE], BF16, kind="ExternalOutput").ap()

    NG = 5
    GOFF = [0, 512, 1024, 1536, 2048]
    GW = [512, 512, 512, 512, 3]
    NE = 67

    with contextlib.ExitStack() as ctx:
        sx = ctx.enter_context(nc.sbuf_tensor("sx", [128, 16 * TE], BF16))
        sx2 = ctx.enter_context(nc.sbuf_tensor("sx2", [128, 16 * 512], BF16))
        sm = ctx.enter_context(nc.sbuf_tensor("sm", [128, TE], F32))
        srs = ctx.enter_context(nc.sbuf_tensor("srs", [128, TE], F32))
        svar = ctx.enter_context(nc.sbuf_tensor("svar", [128, 512], F32))
        sw = ctx.enter_context(nc.sbuf_tensor("sw", [128, 2 * 2048], BF16))
        sasm = ctx.enter_context(nc.sbuf_tensor("sasm", [128, 2 * TE], F32))
        stmp = ctx.enter_context(nc.sbuf_tensor("stmp", [128, 2 * TC], F32))
        sconv = ctx.enter_context(nc.sbuf_tensor("sconv", [128, 2 * TC], BF16))
        sones = ctx.enter_context(nc.sbuf_tensor("sones", [128, 128], BF16))
        scw = ctx.enter_context(nc.sbuf_tensor("scw", [128, 34 * 4], F32))
        scb = ctx.enter_context(nc.sbuf_tensor("scb", [128, 34], F32))
        sdtb = ctx.enter_context(nc.sbuf_tensor("sdtb", [64, 1], F32))
        salog = ctx.enter_context(nc.sbuf_tensor("salog", [64, 1], F32))
        sdta = ctx.enter_context(nc.sbuf_tensor("sdta", [64, TE], F32))
        sdtb2 = ctx.enter_context(nc.sbuf_tensor("sdtb2", [64, TE], F32))
        sdtv = ctx.enter_context(nc.sbuf_tensor("sdtv", [64, TE], F32))
        sneg = ctx.enter_context(nc.sbuf_tensor("sneg", [64, 1], F32))
        scum = ctx.enter_context(nc.sbuf_tensor("scum", [64, TC], F32))
        szero = ctx.enter_context(nc.sbuf_tensor("szero", [64, 128], F32))

        pS1 = ctx.enter_context(nc.psum_tensor("pS1", [128, 512], F32))
        pS2 = ctx.enter_context(nc.psum_tensor("pS2", [128, 512], F32))
        pmm = ctx.enter_context(nc.psum_tensor("pmm", [128, 2 * 512], F32))

        block = ctx.enter_context(nc.Block())

        class NS:
            pass
        E = NS()

        @block.vector
        def _(v):
            E.ve = Eng(ctx, nc, v, "ve")

        @block.scalar
        def _(s):
            E.se = Eng(ctx, nc, s, "se")

        @block.tensor
        def _(t):
            E.pe = Pe(ctx, nc)

        @block.gpsimd
        def _(g):
            E.gq = Dma(ctx, nc, g, "gq")

        @block.sync
        def _(s):
            E.sq = Dma(ctx, nc, s, "sq")

        ve, se, pe, gq, sq = E.ve, E.se, E.pe, E.gq, E.sq
        ldx = Sem(ctx, nc, "ldx")
        ldwb = [Sem(ctx, nc, "ldwb0"), Sem(ctx, nc, "ldwb1")]
        ldc = Sem(ctx, nc, "ldc")
        stxn = Sem(ctx, nc, "stxn")
        stcb = [Sem(ctx, nc, "stcb0"), Sem(ctx, nc, "stcb1")]
        stfin = Sem(ctx, nc, "stfin")

        @block.gpsimd
        def _(g):
            for k in range(16):
                gq.start(sx[:, k * TE:(k + 1) * TE], xt[k * 128:(k + 1) * 128, :], sem=ldx)
            gq.start(sw[:, 0:2048], wT[:, 0:128].rearrange("(k p) e -> p k e", p=128),
                     sem=ldwb[0])

        @block.sync
        def _(s):
            sq.start(scw[:, :], cw, sem=ldc)
            sq.start(scb[:, :], cb, sem=ldc)
            sq.start(sdtb[:], dtb, sem=ldc)
            sq.start(salog[:], alog, sem=ldc)
            sq.start(sones[:], ones_bf, sem=ldc)
        ldc_done = ldc.n

        v_xn_done = None
        for g in range(NG):
            o, w = GOFF[g], GW[g]

            @block.vector
            def _(v, g=g, o=o, w=w):
                if g == 0:
                    ve.wait(ldx)
                for k in range(16):
                    ve.tensor_mul(sx2[:, k * 512:k * 512 + w],
                                  sx[:, k * TE + o:k * TE + o + w],
                                  sx[:, k * TE + o:k * TE + o + w])
            vx2 = ve.n

            @block.tensor
            def _(t, g=g, o=o, w=w, vx2=vx2):
                pe.wait(ldc, ldc_done)
                pe.wait(ve, vx2)
                for k in range(16):
                    pe.matmul(pS1[:, 0:w], sones[:], sx[:, k * TE + o:k * TE + o + w],
                              start=(k == 0), stop=(k == 15))
                for k in range(16):
                    i = pe.matmul(pS2[:, 0:w], sones[:], sx2[:, k * 512:k * 512 + w],
                                  start=(k == 0), stop=(k == 15))
                pe.done(i)
            pest = pe.n

            @block.vector
            def _(v, g=g, o=o, w=w, pest=pest):
                ve.wait(pe, pest)
                ve.tensor_scalar_mul(sm[:, o:o + w], pS1[:, 0:w], 1.0 / D)
                ve.tensor_mul(svar[:, 0:w], sm[:, o:o + w], sm[:, o:o + w])
                ve.scalar_tensor_tensor(svar[:, 0:w], pS2[:, 0:w], 1.0 / D,
                                        svar[:, 0:w], ALU.mult, ALU.subtract)
                ve.tensor_scalar_add(svar[:, 0:w], svar[:, 0:w], LN_EPS)
            vnorm = ve.n

            @block.scalar
            def _(s, g=g, o=o, w=w, vnorm=vnorm):
                se.wait(ve, vnorm)
                se.activation(srs[:, o:o + w], svar[:, 0:w], AF.Ln)
                se.activation(srs[:, o:o + w], srs[:, o:o + w], AF.Exp, scale=-0.5)
            snorm = se.n

            @block.vector
            def _(v, g=g, o=o, w=w, snorm=snorm):
                ve.wait(se, snorm)
                for k in range(16):
                    sl = slice(k * TE + o, k * TE + o + w)
                    ve.tensor_sub(sx[:, sl], sx[:, sl], sm[:, o:o + w])
                    ve.tensor_mul(sx[:, sl], sx[:, sl], srs[:, o:o + w])
            v_xn_done = ve.n

        @block.sync
        def _(s):
            sq.wait(ve, v_xn_done)
            for k in range(16):
                sq.start(xn_o[k * 128:(k + 1) * 128, :], sx[:, k * TE:(k + 1) * TE], sem=stxn)

        pemm = {}
        sev = {}
        sev_tile = {}
        vcv = {}
        scv = {}

        for e in range(NE):
            rows = 64 if e == 66 else 128
            pp = e % 2
            ppn = (e + 1) % 2

            @block.gpsimd
            def _(g, e=e, ppn=ppn):
                if e + 1 < NE:
                    nrows = 64 if e + 1 == 66 else 128
                    if e >= 1:
                        gq.wait(pe, pemm[(e - 1, NG - 1)])
                    gq.start(sw[:, ppn * 2048:ppn * 2048 + 16 * nrows],
                             wT[:, (e + 1) * 128:(e + 1) * 128 + nrows]
                             .rearrange("(k p) e -> p k e", p=128), sem=ldwb[ppn])

            for g in range(NG):
                o, w = GOFF[g], GW[g]
                idx = e * NG + g
                bank = idx % 2

                @block.tensor
                def _(t, e=e, rows=rows, pp=pp, g=g, o=o, w=w, idx=idx, bank=bank):
                    if g == 0:
                        pe.wait(ldwb[e % 2], 16 * (e // 2 + 1))
                        if e == 0:
                            pe.wait(ve, v_xn_done)
                    if idx >= 2:
                        pidx = idx - 2
                        pe.wait(se, sev[(pidx // NG, pidx % NG)])
                    for k in range(16):
                        i = pe.matmul(pmm[0:rows, bank * 512:bank * 512 + w],
                                      sw[:, pp * 2048 + k * rows: pp * 2048 + (k + 1) * rows],
                                      sx[:, k * TE + o:k * TE + o + w],
                                      start=(k == 0), stop=(k == 15))
                    pemm[(e, g)] = pe.done(i)

                @block.scalar
                def _(s, e=e, rows=rows, pp=pp, g=g, o=o, w=w, idx=idx, bank=bank):
                    if g == 0:
                        if e < 32 and e >= 2:
                            se.wait(stcb[e % 2], 16 * (e // 2))
                        if 34 <= e < 66:
                            se.wait(ve, vcv[e - 2])
                    se.wait(pe, pemm[(e, g)])
                    if e < 32:
                        lo = max(o, 3) - 3
                        hi = o + w - 3
                        se.activation(sconv[:, pp * TC + lo:pp * TC + hi],
                                      pmm[:, bank * 512 + (lo + 3 - o):bank * 512 + (hi + 3 - o)],
                                      AF.Silu)
                    elif e == 66:
                        se.copy(sdta[0:64, o:o + w], pmm[0:64, bank * 512:bank * 512 + w])
                    else:
                        se.copy(sasm[:, pp * TE + o:pp * TE + o + w],
                                pmm[:, bank * 512:bank * 512 + w])
                    sev[(e, g)] = se.n
            sev_tile[e] = se.n

            if e < 32:
                @block.sync
                def _(s, e=e, pp=pp):
                    sq.wait(se, sev_tile[e])
                    sq.start(zs_o[e * 128:(e + 1) * 128, :], sconv[:, pp * TC:(pp + 1) * TC],
                             sem=stcb[pp])
            elif e < 66:
                ct = e - 32

                @block.vector
                def _(v, e=e, pp=pp, ct=ct):
                    ve.wait(se, sev_tile[e])
                    if e >= 34:
                        ve.wait(se, scv[e - 2])
                    a = sasm[:, pp * TE:(pp + 1) * TE]
                    tb = stmp[:, pp * TC:(pp + 1) * TC]
                    ve.tensor_scalar(tb, a[:, 0:TC], scw[:, 4 * ct:4 * ct + 1],
                                     scb[:, ct:ct + 1], ALU.mult, ALU.add)
                    for k in range(1, 4):
                        ve.scalar_tensor_tensor(tb, a[:, k:TC + k], scw[:, 4 * ct + k:4 * ct + k + 1],
                                                tb, ALU.mult, ALU.add)
                    vcv[e] = ve.n

                @block.scalar
                def _(s, e=e, pp=pp):
                    se.wait(ve, vcv[e])
                    if e >= 2:
                        se.wait(stcb[e % 2], 16 * (e // 2))
                    se.activation(sconv[:, pp * TC:(pp + 1) * TC],
                                  stmp[:, pp * TC:(pp + 1) * TC], AF.Silu)
                    scv[e] = se.n

                @block.sync
                def _(s, e=e, pp=pp):
                    sq.wait(se, scv[e])
                    if e < 64:
                        dst = xc_o[(e - 32) * 128:(e - 31) * 128, :]
                    elif e == 64:
                        dst = bf_o
                    else:
                        dst = cf_o
                    sq.start(dst, sconv[:, pp * TC:(pp + 1) * TC], sem=stcb[pp])

        @block.scalar
        def _(s):
            se.wait(ldc, ldc_done)
            se.activation(sdtb2[0:64, :], sdta[0:64, :], AF.Abs, bias=sdtb[:])
            se.activation(sdtb2[0:64, :], sdtb2[0:64, :], AF.Exp, scale=-1.0)
            se.activation(sdtb2[0:64, :], sdtb2[0:64, :], AF.Ln, bias=1.0)
            se.activation(sdta[0:64, :], sdta[0:64, :], AF.Relu, bias=sdtb[:])
            se.activation(sneg[:], salog[:], AF.Exp)
        s_dt = se.n

        @block.vector
        def _(v):
            ve.wait(se, s_dt)
            ve.tensor_add(sdtv[0:64, :], sdta[0:64, :], sdtb2[0:64, :])
            ve.tensor_scalar_mul(sneg[:], sneg[:], -1.0)
            ve.memset(szero[:], 0.0)
            ve.tensor_scalar_mul(sdtb2[0:64, 0:TC], sdtv[0:64, 3:3 + TC], sneg[:])
            for c in range(TC // CHUNK):
                ve.tensor_tensor_scan(scum[:, c * CHUNK:(c + 1) * CHUNK],
                                      sdtb2[0:64, c * CHUNK:(c + 1) * CHUNK],
                                      szero[:], 0.0, ALU.add, ALU.add)
        v_dt = ve.n

        @block.sync
        def _(s):
            sq.wait(ve, v_dt)
            sq.start(dtv_o, sdtv[0:64, 3:3 + TC], sem=stfin)
            sq.start(cuma_o, scum[0:64, :], sem=stfin)
            for sem in (stfin, stxn, stcb[0], stcb[1], ldc):
                s.wait_ge(sem.h, sem.n)

        @block.gpsimd
        def _(g):
            for sem in (ldx, ldwb[0], ldwb[1]):
                g.wait_ge(sem.h, sem.n)

    return nc


def host_mamba_front(u_flat, in_proj_w, conv_w, conv_b, dt_bias, A_log):
    wT = _f32(in_proj_w.T)
    ones_bf = np.ones((128, 128), dtype=BF)
    cw_t = _f32(conv_w.reshape(34, 128, 4).transpose(1, 0, 2).reshape(128, 136))
    cb_t = _f32(conv_b.reshape(34, 128).T)
    in_maps = []
    for c in range(NCORES):
        s0 = c * TC
        bstart = (s0 // T) * T
        nh = min(3, s0 - bstart)
        ext = np.zeros((TC + 3, D), np.float32)
        if nh:
            ext[3 - nh:3] = u_flat[s0 - nh:s0]
        ext[3:] = u_flat[s0:s0 + TC]
        in_maps.append({
            "xt": _f32(ext.T),
            "wT": wT,
            "cw": cw_t,
            "cb": cb_t,
            "dtb": _f32(dt_bias.reshape(-1, 1)),
            "alog": _f32(A_log.reshape(-1, 1)),
            "ones_bf": ones_bf,
        })
    return in_maps




def build_ssd():
    nc = bass.Bass("TRN2", target_bir_lowering=False)
    NCH = TC // CHUNK
    HP = D_INNER
    x_pm = nc.dram_tensor("x_pm", [TC, HP], BF16, kind="ExternalInput").ap()
    b_pm = nc.dram_tensor("b_pm", [TC, D_STATE], BF16, kind="ExternalInput").ap()
    b_fm = nc.dram_tensor("b_fm", [D_STATE, TC], BF16, kind="ExternalInput").ap()
    c_fm = nc.dram_tensor("c_fm", [D_STATE, TC], BF16, kind="ExternalInput").ap()
    dtv_pm = nc.dram_tensor("dtv_pm", [TC, NHEADS], F32, kind="ExternalInput").ap()
    cuma_pm = nc.dram_tensor("cuma_pm", [TC, NHEADS], F32, kind="ExternalInput").ap()
    culast_c = nc.dram_tensor("culast_c", [1, NCH * 64], F32, kind="ExternalInput").ap()
    culast_h = nc.dram_tensor("culast_h", [1, 64 * NCH], F32, kind="ExternalInput").ap()
    crow = nc.dram_tensor("crow", [NCH, 64 * 128], F32, kind="ExternalInput").ap()
    drow = nc.dram_tensor("drow", [1, 64], F32, kind="ExternalInput").ap()
    trilm = nc.dram_tensor("trilm", [128, 128], F32, kind="ExternalInput").ap()
    onecol = nc.dram_tensor("onecol", [1, 128], F32, kind="ExternalInput").ap()
    bmask = nc.dram_tensor("bmask", [8, 8], F32, kind="ExternalInput").ap()
    pmask = nc.dram_tensor("pmask", [1, 8], F32, kind="ExternalInput").ap()

    y_o = nc.dram_tensor("y_pm", [TC, HP], BF16, kind="ExternalOutput").ap()

    cc_S = nc.dram_tensor("cc_S", [D_STATE, HP], F32).ap()
    cc_sl = nc.dram_tensor("cc_sl", [1, 64], F32).ap()
    ag_S = nc.dram_tensor("ag_S", [NCORES * D_STATE, HP], F32).ap()
    ag_sl = nc.dram_tensor("ag_sl", [NCORES, 64], F32).ap()

    with contextlib.ExitStack() as ctx:
        sB_pm = ctx.enter_context(nc.sbuf_tensor("sB_pm", [128, NCH * 128], BF16))
        sB_fm = ctx.enter_context(nc.sbuf_tensor("sB_fm", [128, TC], BF16))
        sC_fm = ctx.enter_context(nc.sbuf_tensor("sC_fm", [128, TC], BF16))
        sdtv = ctx.enter_context(nc.sbuf_tensor("sdtv", [128, NCH * 64], F32))
        scuma = ctx.enter_context(nc.sbuf_tensor("scuma", [128, NCH * 64], F32))
        sculc = ctx.enter_context(nc.sbuf_tensor("sculc", [1, NCH * 64], F32))
        sculh = ctx.enter_context(nc.sbuf_tensor("sculh", [1, 64 * NCH], F32))
        srowp = ctx.enter_context(nc.sbuf_tensor("srowp", [1, 64 * 128], F32))
        sdrow = ctx.enter_context(nc.sbuf_tensor("sdrow", [1, 64], F32))
        stril = ctx.enter_context(nc.sbuf_tensor("stril", [128, 128], F32))
        sonec = ctx.enter_context(nc.sbuf_tensor("sonec", [1, 128], F32))
        sbm = ctx.enter_context(nc.sbuf_tensor("sbm", [8, 8], F32))
        spm = ctx.enter_context(nc.sbuf_tensor("spm", [1, 8], F32))
        sx = ctx.enter_context(nc.sbuf_tensor("sx", [128, 2 * HP], BF16))
        sxdt = ctx.enter_context(nc.sbuf_tensor("sxdt", [128, HP], BF16))
        sxdt2 = ctx.enter_context(nc.sbuf_tensor("sxdt2", [128, HP], BF16))
        sS = ctx.enter_context(nc.sbuf_tensor("sS", [128, HP], F32))
        sSb = ctx.enter_context(nc.sbuf_tensor("sSb", [128, HP], BF16))
        sdec = ctx.enter_context(nc.sbuf_tensor("sdec", [128, 64], F32))
        sds = ctx.enter_context(nc.sbuf_tensor("sds", [128, 64], F32))
        sdd = ctx.enter_context(nc.sbuf_tensor("sdd", [128, 64], F32))
        sDbc = ctx.enter_context(nc.sbuf_tensor("sDbc", [128, 64], F32))
        seA = ctx.enter_context(nc.sbuf_tensor("seA", [128, 64], F32))
        sGtm = ctx.enter_context(nc.sbuf_tensor("sGtm", [128, 128], BF16))
        sLt = ctx.enter_context(nc.sbuf_tensor("sLt", [128, 1024], F32))
        sMt = ctx.enter_context(nc.sbuf_tensor("sMt", [128, 1024], BF16))
        syo = ctx.enter_context(nc.sbuf_tensor("syo", [128, 512], F32))
        sy = ctx.enter_context(nc.sbuf_tensor("sy", [128, 3 * 512], BF16))
        ssum = ctx.enter_context(nc.sbuf_tensor("ssum", [1, 64], F32))
        srow = ctx.enter_context(nc.sbuf_tensor("srow", [1, 64], F32))
        sSin = ctx.enter_context(nc.sbuf_tensor("sSin", [128, HP], F32))
        scoef = ctx.enter_context(nc.sbuf_tensor("scoef", [128, 64], F32))
        sfac = ctx.enter_context(nc.sbuf_tensor("sfac", [128, 64], F32))
        sagsl = ctx.enter_context(nc.sbuf_tensor("sagsl", [8, 64], F32))
        scexp = ctx.enter_context(nc.sbuf_tensor("scexp", [128, 64], F32))
        sagS = sx[:, 0:2 * HP].bitcast(F32)
        sSinb = sxdt[:, 0:HP]
        stmp = ctx.enter_context(nc.sbuf_tensor("stmp", [128, 2048], F32))

        pst = ctx.enter_context(nc.psum_tensor("pst", [128, 1024], F32))
        pLt = ctx.enter_context(nc.psum_tensor("pLt", [128, 1024], F32))
        pYd = ctx.enter_context(nc.psum_tensor("pYd", [128, 512], F32))
        pYo = ctx.enter_context(nc.psum_tensor("pYo", [128, 512], F32))
        pbcC = ctx.enter_context(nc.psum_tensor("pbcC", [128, 512], F32))
        pGt = ctx.enter_context(nc.psum_tensor("pGt", [128, 512], F32))

        block = ctx.enter_context(nc.Block())

        class NS:
            pass
        E = NS()

        @block.vector
        def _(v):
            E.ve = Eng(ctx, nc, v, "ve")

        @block.scalar
        def _(s):
            E.se = Eng(ctx, nc, s, "se")

        @block.tensor
        def _(t):
            E.pe = Pe(ctx, nc)

        @block.gpsimd
        def _(g):
            E.gq = Dma(ctx, nc, g, "gq")
            E.ge = Eng(ctx, nc, g, "ge")

        @block.sync
        def _(s):
            E.sq = Dma(ctx, nc, s, "sq")

        ve, se, pe, gq, ge, sq = E.ve, E.se, E.pe, E.gq, E.ge, E.sq
        ldc = Sem(ctx, nc, "ldc2")
        ldxb = [Sem(ctx, nc, "ldxb0"), Sem(ctx, nc, "ldxb1")]
        styp = [Sem(ctx, nc, "styp0"), Sem(ctx, nc, "styp1")]
        stS = Sem(ctx, nc, "stS")
        agc = Sem(ctx, nc, "agc")
        agl = Sem(ctx, nc, "agl")
        ags = Sem(ctx, nc, "ags")
        sty2 = [Sem(ctx, nc, "sty2a"), Sem(ctx, nc, "sty2b")]
        ldrow = Sem(ctx, nc, "ldrow")

        @block.sync
        def _(s):
            for k in range(NCH):
                sq.start(sB_pm[:, k * 128:(k + 1) * 128], b_pm[k * 128:(k + 1) * 128, :], sem=ldc)
            sq.start(sB_fm[:, :], b_fm, sem=ldc)
            sq.start(sC_fm[:, :], c_fm, sem=ldc)
            for k in range(NCH):
                sq.start(sdtv[:, k * 64:(k + 1) * 64], dtv_pm[k * 128:(k + 1) * 128, :], sem=ldc)
                sq.start(scuma[:, k * 64:(k + 1) * 64], cuma_pm[k * 128:(k + 1) * 128, :], sem=ldc)
            sq.start(sculc[:, :], culast_c, sem=ldc)
            sq.start(sculh[:, :], culast_h, sem=ldc)
            sq.start(sdrow[:, :], drow, sem=ldc)
            sq.start(stril[:, :], trilm, sem=ldc)
            sq.start(sonec[:, :], onecol, sem=ldc)
            sq.start(sbm[:, :], bmask, sem=ldc)
            sq.start(spm[:, :], pmask, sem=ldc)
        ld_all = ldc.n

        @block.gpsimd
        def _(g):
            gq.start(sx[:, 0:HP], x_pm[0:128, :], sem=ldxb[0])

        @block.tensor
        def _(t):
            pe.wait(ldc, ld_all)
            i = pe.matmul(pGt[:, 0:64], sonec[:, :], sdrow[:, :], start=True, stop=True)
            pe.done(i)
        pe_dbc = pe.n

        @block.vector
        def _(v):
            ve.wait(pe, pe_dbc)
            ve.tensor_copy(sDbc[:, :], pGt[:, 0:64])
            ve.memset(sS[:, :], 0.0)
        v_init = ve.n

        K = {}
        for c in range(NCH):
            pp = c % 2
            ppn = (c + 1) % 2
            cg = slice(c * 64, (c + 1) * 64)

            @block.gpsimd
            def _(g, c=c, ppn=ppn):
                if c + 1 < NCH:
                    if c >= 1:
                        gq.wait(ve, K[(c - 1, "xdt")])
                        gq.wait(ge, K[(c - 1, 7, "gd")])
                    gq.start(sx[:, ppn * HP:(ppn + 1) * HP],
                             x_pm[(c + 1) * 128:(c + 2) * 128, :], sem=ldxb[ppn])
                if c > 0:
                    gq.wait(pe, K[(c - 1, 7, "lt")])
                gq.start(srowp[0:1, :], crow[c:c + 1, :], sem=ldrow)

            @block.tensor
            def _(t, c=c):
                if c == 0:
                    pe.wait(ve, v_init)
                else:
                    pe.wait(se, K[(c - 1, "dec")])
                    pe.wait(ve, K[(c - 1, "ds")])
                i = pe.matmul(pbcC[:, 0:64], sonec[:, :],
                              sculc[:, c * 64:(c + 1) * 64], start=True, stop=True)
                K[(c, "bc")] = pe.done(i)

            @block.vector
            def _(v, c=c, cg=cg):
                ve.wait(pe, K[(c, "bc")])
                ve.tensor_sub(sds[:, :], pbcC[:, 0:64], scuma[:, cg])
            K[(c, "ds")] = ve.n

            @block.scalar
            def _(s, c=c, cg=cg):
                se.wait(ve, K[(c, "ds")])
                se.activation(sdec[:, :], pbcC[:, 0:64], AF.Exp)
                se.activation(sds[:, :], sds[:, :], AF.Exp)
                se.activation(seA[:, :], scuma[:, cg], AF.Exp)
            K[(c, "dec")] = se.n

            @block.vector
            def _(v, c=c, cg=cg, pp=pp):
                ve.wait(se, K[(c, "dec")])
                ve.tensor_mul(sdd[:, :], sds[:, :], sdtv[:, cg])
                if c == 0:
                    ve.wait(ldxb[0], 16)
                else:
                    ve.wait(ldxb[pp], 16 * (c // 2 + 1))
                xin = sx[:, pp * HP:(pp + 1) * HP].rearrange("p (h q) -> p h q", h=64)
                ve.tensor_tensor(sxdt.rearrange("p (h q) -> p h q", h=64), xin,
                                 sdtv[:, cg].unsqueeze(2).broadcast_to((128, 64, 64)), ALU.mult)
                ve.tensor_tensor(sxdt2.rearrange("p (h q) -> p h q", h=64), xin,
                                 sdd[:, :].unsqueeze(2).broadcast_to((128, 64, 64)), ALU.mult)
            K[(c, "xdt")] = ve.n

            @block.scalar
            def _(s, c=c):
                se.wait(ve, K[(c, "xdt")])
                se.copy(sSb[:, :], sS[:, :])
            K[(c, "sbf")] = se.n

            @block.tensor
            def _(t, c=c):
                pe.wait(se, K[(c, "sbf")])
                if c > 0:
                    pe.wait(ve, K[(c - 1, "gtm")])
                i = pe.matmul(pGt[:, 128:256], sB_fm[:, c * 128:(c + 1) * 128],
                              sC_fm[:, c * 128:(c + 1) * 128], start=True, stop=True)
                K[(c, "gt")] = pe.done(i)

            @block.vector
            def _(v, c=c):
                ve.wait(pe, K[(c, "gt")])
                ve.tensor_mul(sGtm[:, :], pGt[:, 128:256], stril[:, :])
            K[(c, "gtm")] = ve.n

            for hg in range(8):
                idx = c * 8 + hg
                par = idx % 2
                hsl = slice(hg * 512, (hg + 1) * 512)

                @block.tensor
                def _(t, c=c, hg=hg):
                    if hg == 0:
                        pe.wait(ve, K[(c, "gtm")])
                        pe.wait(ldrow, 16 * (c + 1))
                    prev = K.get((c, hg - 1, "ltsub") if hg else (c - 1, 7, "ltsub"))
                    if prev:
                        pe.wait(ve, prev)
                    for j in range(2):
                        i = pe.matmul(pLt[:, j * 512:(j + 1) * 512], sonec[:, :],
                                      srowp[0:1, hg * 1024 + j * 512:hg * 1024 + (j + 1) * 512],
                                      start=True, stop=True)
                    K[(c, hg, "lt")] = pe.done(i)

                @block.vector
                def _(v, c=c, hg=hg):
                    ve.wait(pe, K[(c, hg, "lt")])
                    ve.tensor_tensor(sLt.rearrange("p (h q) -> p h q", h=8),
                                     scuma[:, c * 64 + hg * 8:c * 64 + (hg + 1) * 8]
                                     .unsqueeze(2).broadcast_to((128, 8, 128)),
                                     pLt.rearrange("p (h q) -> p h q", h=8),
                                     ALU.subtract)
                K[(c, hg, "ltsub")] = ve.n

                @block.scalar
                def _(s, c=c, hg=hg):
                    se.wait(ve, K[(c, hg, "ltsub")])
                    if K.get((c, hg - 1, "mt") if hg else (c - 1, 7, "mt")):
                        se.wait(ve, K[(c, hg - 1, "mt") if hg else (c - 1, 7, "mt")])
                    se.activation(sLt[:, :], sLt[:, :], AF.Relu)
                    se.activation(sMt[:, :], sLt[:, :], AF.Exp, scale=-1.0)
                K[(c, hg, "srelu")] = se.n

                @block.vector
                def _(v, c=c, hg=hg):
                    ve.wait(se, K[(c, hg, "srelu")])
                    ve.tensor_tensor(sMt.rearrange("p (h q) -> p h q", h=8),
                                     sMt.rearrange("p (h q) -> p h q", h=8),
                                     sGtm[:, :].unsqueeze(1).broadcast_to((128, 8, 128)),
                                     ALU.mult)
                K[(c, hg, "mt")] = ve.n

                @block.tensor
                def _(t, c=c, hg=hg, hsl=hsl):
                    pe.wait(ve, K[(c, hg, "mt")])
                    prev_vy = K.get((c, hg - 1, "vy") if hg else (c - 1, 7, "vy"))
                    if prev_vy:
                        pe.wait(ve, prev_vy)
                    for h in range(8):
                        i = pe.matmul(pYd[:, h * 64:(h + 1) * 64], sMt[:, h * 128:(h + 1) * 128],
                                      sxdt[:, hg * 512 + h * 64:hg * 512 + (h + 1) * 64],
                                      start=True, stop=True)
                    pe.done(i)
                    i = pe.matmul(pYo[:, :], sC_fm[:, c * 128:(c + 1) * 128],
                                  sSb[:, hsl], start=True, stop=True)
                    K[(c, hg, "yo")] = pe.done(i)

                @block.gpsimd
                def _(g, c=c, hg=hg, pp=pp):
                    ge.wait(ve, K[(c, "xdt")])
                    prev_vy = K.get((c, hg - 1, "vy") if hg else (c - 1, 7, "vy"))
                    if prev_vy:
                        ge.wait(ve, prev_vy)
                    ge.tensor_tensor(sy[:, 1024:1536].rearrange("p (h q) -> p h q", h=8),
                                     sx[:, pp * HP + hg * 512:pp * HP + (hg + 1) * 512]
                                     .rearrange("p (h q) -> p h q", h=8),
                                     sDbc[:, hg * 8:(hg + 1) * 8].unsqueeze(2).broadcast_to((128, 8, 64)),
                                     ALU.mult)
                K[(c, hg, "gd")] = ge.n

                @block.vector
                def _(v, c=c, hg=hg, par=par, idx=idx):
                    ve.wait(pe, K[(c, hg, "yo")])
                    ve.wait(ge, K[(c, hg, "gd")])
                    if idx >= 2:
                        ve.wait(styp[par], 16 * (idx // 2))
                    ve.tensor_tensor(syo.rearrange("p (h q) -> p h q", h=8),
                                     pYo[:, :].rearrange("p (h q) -> p h q", h=8),
                                     seA[:, hg * 8:(hg + 1) * 8].unsqueeze(2).broadcast_to((128, 8, 64)),
                                     ALU.mult)
                    ve.tensor_add(syo[:, :], syo[:, :], pYd[:, :])
                    ve.tensor_add(sy[:, par * 512:(par + 1) * 512], syo[:, :], sy[:, 1024:1536])
                K[(c, hg, "vy")] = ve.n

                @block.sync
                def _(s, c=c, hg=hg, par=par):
                    sq.wait(ve, K[(c, hg, "vy")])
                    sq.start(y_o[c * 128:(c + 1) * 128, hg * 512:(hg + 1) * 512],
                             sy[:, par * 512:(par + 1) * 512], sem=styp[par])

            for qt in range(4):
                hs = slice(qt * 1024, (qt + 1) * 1024)

                @block.tensor
                def _(t, c=c, qt=qt, hs=hs):
                    if qt == 0:
                        pe.wait(ve, K[(c, 7, "vy")])
                    else:
                        pe.wait(ve, K[(c, qt - 1, "supd")])
                    for j in range(2):
                        i = pe.matmul(pst[:, j * 512:(j + 1) * 512],
                                      sB_pm[:, c * 128:(c + 1) * 128],
                                      sxdt2[:, qt * 1024 + j * 512:qt * 1024 + (j + 1) * 512],
                                      start=True, stop=True)
                    K[(c, qt, "st")] = pe.done(i)

                @block.gpsimd
                def _(g, c=c, qt=qt, hs=hs):
                    ge.wait(se, K[(c, "sbf")])
                    if K.get((c, qt - 1, "supd") if qt else (c - 1, 3, "supd")):
                        ge.wait(ve, K[(c, qt - 1, "supd") if qt else (c - 1, 3, "supd")])
                    ge.tensor_tensor(stmp[:, 0:1024].rearrange("p (h q) -> p h q", h=16),
                                     sS[:, hs].rearrange("p (h q) -> p h q", h=16),
                                     sdec[:, qt * 16:(qt + 1) * 16].unsqueeze(2)
                                     .broadcast_to((128, 16, 64)), ALU.mult)
                K[(c, qt, "gs")] = ge.n

                @block.vector
                def _(v, c=c, qt=qt, hs=hs):
                    ve.wait(pe, K[(c, qt, "st")])
                    ve.wait(ge, K[(c, qt, "gs")])
                    ve.tensor_add(sS[:, hs], stmp[:, 0:1024], pst[:, :])
                K[(c, qt, "supd")] = ve.n

        @block.vector
        def _(v):
            ve.tensor_reduce(ssum[0:1, :], sculh[:, :].rearrange("p (h c) -> p h c", h=64),
                             mybir.AxisListType.X, ALU.add)
        v_sl = ve.n

        @block.sync
        def _(s):
            sq.wait(ve, v_sl)
            sq.wait(ve, K[(NCH - 1, 3, "supd")])
            sq.start(cc_S, sS[:, :], sem=stS)
            sq.start(cc_sl, ssum[0:1, :], sem=stS)

        @block.gpsimd
        def _(g):
            g.wait_ge(stS.h, stS.n)
            gq.collective("AllGather", ALU.bypass, [list(range(NCORES))],
                          [cc_S.opt()], [ag_S.opt()], sem=agc)
            gq.collective("AllGather", ALU.bypass, [list(range(NCORES))],
                          [cc_sl.opt()], [ag_sl.opt()], sem=agc)

        @block.sync
        def _(s):
            s.wait_ge(agc.h, agc.n)
            sq.start(sagsl[:, :], ag_sl, sem=agl)

        @block.vector
        def _(v):
            ve.wait(agl, agl.n)
            ve.memset(sSin[:, :], 0.0)
        v_sin0 = ve.n

        R = {}
        for j in range(NCORES):
            @block.tensor
            def _(t, j=j):
                if j == 0:
                    pe.wait(ve, v_sin0)
                else:
                    pe.wait(ve, R[(j - 1, "acc")])
                i = pe.matmul(pbcC[0:1, 0:64], sbm[:, j:j + 1], sagsl[:, :],
                              start=True, stop=True)
                R[(j, "log")] = pe.done(i)

            @block.vector
            def _(v, j=j):
                ve.wait(pe, R[(j, "log")])
                ve.tensor_copy(srow[0:1, :], pbcC[0:1, 0:64])
            R[(j, "vrow")] = ve.n

            @block.scalar
            def _(s, j=j):
                se.wait(ve, R[(j, "vrow")])
                se.activation(srow[0:1, :], srow[0:1, :], AF.Exp)
            R[(j, "sexp")] = se.n

            @block.vector
            def _(v, j=j):
                ve.wait(se, R[(j, "sexp")])
                ve.tensor_scalar_mul(srow[0:1, :], srow[0:1, :], spm[0:1, j:j + 1])
            R[(j, "vpm")] = ve.n

            @block.tensor
            def _(t, j=j):
                pe.wait(ve, R[(j, "vpm")])
                i = pe.matmul(pbcC[:, 64:128], sonec[:, :], srow[0:1, :], start=True, stop=True)
                R[(j, "bc")] = pe.done(i)

            @block.vector
            def _(v, j=j):
                ve.wait(pe, R[(j, "bc")])
                ve.tensor_copy(scexp[:, :], pbcC[:, 64:128])
            R[(j, "cexp")] = ve.n

            @block.sync
            def _(s, j=j):
                sq.wait(ve, R[(j, "cexp")])
                sq.start(sagS, ag_S[j * 128:(j + 1) * 128, :], sem=ags)
            R[(j, "lds")] = ags.n

            @block.vector
            def _(v, j=j):
                ve.wait(ags, R[(j, "lds")])
                ve.tensor_tensor(sagS.rearrange("p (h q) -> p h q", h=64),
                                 sagS.rearrange("p (h q) -> p h q", h=64),
                                 scexp[:, :].unsqueeze(2).broadcast_to((128, 64, 64)),
                                 ALU.mult)
                ve.tensor_add(sSin[:, :], sSin[:, :], sagS)
            R[(j, "acc")] = ve.n

        @block.scalar
        def _(s):
            se.wait(ve, R[(NCORES - 1, "acc")])
            se.copy(sSinb, sSin[:, :])
        s_sinb = se.n

        @block.vector
        def _(v):
            ve.wait(se, s_sinb)
            ve.memset(scoef[:, :], 1.0)
        v_coef0 = ve.n

        P = {}
        for c in range(NCH):
            @block.tensor
            def _(t, c=c):
                if c == 0:
                    pe.wait(ve, v_coef0)
                else:
                    pe.wait(ve, P[(c - 1, "coef")])
                    pe.wait(se, P[(c - 1, "dec")])
                i = pe.matmul(pbcC[:, 0:64], sonec[:, :],
                              sculc[:, c * 64:(c + 1) * 64], start=True, stop=True)
                P[(c, "bc")] = pe.done(i)

            @block.scalar
            def _(s, c=c):
                se.wait(pe, P[(c, "bc")])
                se.activation(sdec[:, :], pbcC[:, 0:64], AF.Exp)
                se.activation(seA[:, :], scuma[:, c * 64:(c + 1) * 64], AF.Exp)
            P[(c, "dec")] = se.n

            @block.vector
            def _(v, c=c):
                ve.wait(se, P[(c, "dec")])
                ve.tensor_mul(sfac[:, :], seA[:, :], scoef[:, :])
            P[(c, "fac")] = ve.n

            for hg in range(8):
                idx = c * 8 + hg
                par = idx % 2

                @block.tensor
                def _(t, c=c, hg=hg):
                    if hg == 0:
                        pe.wait(ve, P[(c, "fac")])
                    prev = P.get((c, hg - 1, "vy") if hg else (c - 1, 7, "vy"))
                    if prev:
                        pe.wait(ve, prev)
                    i = pe.matmul(pYo[:, :], sC_fm[:, c * 128:(c + 1) * 128],
                                  sSinb[:, hg * 512:(hg + 1) * 512], start=True, stop=True)
                    P[(c, hg, "mm")] = pe.done(i)

                @block.vector
                def _(v, c=c, hg=hg, par=par, idx=idx):
                    ve.wait(pe, P[(c, hg, "mm")])
                    if idx >= 2:
                        ve.wait(sty2[par], 16 * (idx // 2))
                    ve.tensor_tensor(sy[:, par * 512:(par + 1) * 512].rearrange("p (h q) -> p h q", h=8),
                                     pYo[:, :].rearrange("p (h q) -> p h q", h=8),
                                     sfac[:, hg * 8:(hg + 1) * 8].unsqueeze(2)
                                     .broadcast_to((128, 8, 64)), ALU.mult)
                P[(c, hg, "vy")] = ve.n

                @block.gpsimd
                def _(g, c=c, hg=hg, par=par):
                    ge.wait(ve, P[(c, hg, "vy")])
                    gq.start(y_o[c * 128:(c + 1) * 128, hg * 512:(hg + 1) * 512],
                             sy[:, par * 512:(par + 1) * 512], sem=sty2[par],
                             accum_op=ALU.add)

            @block.vector
            def _(v, c=c):
                ve.wait(se, P[(c, "dec")])
                ve.tensor_mul(scoef[:, :], scoef[:, :], sdec[:, :])
            P[(c, "coef")] = ve.n

        @block.sync
        def _(s):
            for sem in (styp[0], styp[1], stS, ldc, agl, ags):
                s.wait_ge(sem.h, sem.n)

        @block.gpsimd
        def _(g):
            for sem in (sty2[0], sty2[1], agc, ldxb[0], ldxb[1]):
                g.wait_ge(sem.h, sem.n)

    return nc


def host_ssd(xc_all, bf_all, cf_all, dtv_all, cuma_all, Dvec):
    NCH = TC // CHUNK
    tril = (np.arange(128)[None, :] >= np.arange(128)[:, None]).astype(np.float32)
    onec = np.ones((1, 128), np.float32)
    drow = _f32(Dvec.reshape(1, 64))
    in_maps = []
    for c in range(NCORES):
        cumA = np.asarray(cuma_all[c], np.float32)
        dtv = np.asarray(dtv_all[c], np.float32)
        culast = cumA[:, CHUNK - 1::CHUNK]
        cr = np.zeros((NCH, 64 * 128), np.float32)
        for ch in range(NCH):
            cr[ch] = cumA[:, ch * 128:(ch + 1) * 128].reshape(-1)
        p = c % 4
        b0 = (c // 4) * 4
        bm = np.zeros((8, 8), np.float32)
        pm = np.zeros((1, 8), np.float32)
        for src in range(8):
            if b0 <= src < b0 + p:
                pm[0, src] = 1.0
                for k in range(src + 1, b0 + p):
                    bm[k, src] = 1.0
        in_maps.append({
            "x_pm": _bf(np.asarray(xc_all[c]).T),
            "b_pm": _bf(np.asarray(bf_all[c]).T),
            "b_fm": _bf(bf_all[c]),
            "c_fm": _bf(cf_all[c]),
            "dtv_pm": _f32(dtv.T),
            "cuma_pm": _f32(cumA.T),
            "culast_c": _f32(culast.T.reshape(1, -1)),
            "culast_h": _f32(culast.reshape(1, -1)),
            "crow": cr,
            "drow": drow,
            "trilm": tril, "onecol": onec,
            "bmask": bm, "pmask": pm,
        })
    return in_maps


def build_mgate():
    nc = bass.Bass("TRN2", target_bir_lowering=False)
    y_fm = nc.dram_tensor("y_fm", [D_INNER, TC], BF16, kind="ExternalInput").ap()
    zs = nc.dram_tensor("zs", [D_INNER, TC], BF16, kind="ExternalInput").ap()
    u_fm = nc.dram_tensor("u_fm", [D, TC], F32, kind="ExternalInput").ap()
    woT = nc.dram_tensor("woT", [D_INNER, D], F32, kind="ExternalInput").ap()
    normw = nc.dram_tensor("normw", [128, 32], F32, kind="ExternalInput").ap()
    ones_bf = nc.dram_tensor("ones_bf", [128, 128], BF16, kind="ExternalInput").ap()
    x1_o = nc.dram_tensor("x1_fm", [D, TC], F32, kind="ExternalOutput").ap()

    NG = 4
    GWD = 512
    NK = 32
    NO = 16

    with contextlib.ExitStack() as ctx:
        sg = ctx.enter_context(nc.sbuf_tensor("sg", [128, NK * GWD], BF16))
        sy = ctx.enter_context(nc.sbuf_tensor("syb", [128, 2 * GWD], BF16))
        sz = ctx.enter_context(nc.sbuf_tensor("szb", [128, 2 * GWD], BF16))
        sg2 = ctx.enter_context(nc.sbuf_tensor("sg2", [128, 2 * GWD], BF16))
        srms = ctx.enter_context(nc.sbuf_tensor("srms", [128, GWD], F32))
        snw = ctx.enter_context(nc.sbuf_tensor("snw", [128, 32], F32))
        sones = ctx.enter_context(nc.sbuf_tensor("sones", [128, 128], BF16))
        sw = ctx.enter_context(nc.sbuf_tensor("sw", [128, 2 * NK * 128], BF16))
        su = ctx.enter_context(nc.sbuf_tensor("su", [128, 2 * GWD], F32))
        sx1 = ctx.enter_context(nc.sbuf_tensor("sx1", [128, 2 * GWD], F32))

        pS = ctx.enter_context(nc.psum_tensor("pS", [128, 512], F32))
        pmm = ctx.enter_context(nc.psum_tensor("pmm", [128, 2 * 512], F32))

        block = ctx.enter_context(nc.Block())

        class NS:
            pass
        E = NS()

        @block.vector
        def _(v):
            E.ve = Eng(ctx, nc, v, "ve")

        @block.scalar
        def _(s):
            E.se = Eng(ctx, nc, s, "se")

        @block.tensor
        def _(t):
            E.pe = Pe(ctx, nc)

        @block.gpsimd
        def _(g):
            E.gq = Dma(ctx, nc, g, "gq")

        @block.sync
        def _(s):
            E.sq = Dma(ctx, nc, s, "sq")

        ve, se, pe, gq, sq = E.ve, E.se, E.pe, E.gq, E.sq
        ldc = Sem(ctx, nc, "ldc")
        ldyz = [Sem(ctx, nc, "ldyz0"), Sem(ctx, nc, "ldyz1")]
        ldw = [Sem(ctx, nc, "ldw0"), Sem(ctx, nc, "ldw1")]
        ldu = [Sem(ctx, nc, "ldu0"), Sem(ctx, nc, "ldu1")]
        stx = [Sem(ctx, nc, "stx0"), Sem(ctx, nc, "stx1")]

        @block.sync
        def _(s):
            sq.start(snw[:, :], normw, sem=ldc)
            sq.start(sones[:, :], ones_bf, sem=ldc)
        ldc_done = ldc.n

        K = {}
        nload = [0, 0]
        nwload = [0, 0]
        nuload = [0, 0]
        nxst = [0, 0]
        for grp in range(NG):
            go = grp * GWD
            for k in range(NK):
                pp = k % 2

                @block.sync
                def _(s, grp=grp, go=go, k=k, pp=pp):
                    if K.get(("g", grp, k - 2)) is not None:
                        sq.wait(ve, K[("g2", grp, k - 2)])
                    sq.start(sy[:, pp * GWD:(pp + 1) * GWD],
                             y_fm[k * 128:(k + 1) * 128, go:go + GWD], sem=ldyz[pp])
                    sq.start(sz[:, pp * GWD:(pp + 1) * GWD],
                             zs[k * 128:(k + 1) * 128, go:go + GWD], sem=ldyz[pp])
                    nload[pp] += 32

                @block.vector
                def _(v, grp=grp, k=k, pp=pp):
                    ve.wait(ldyz[pp], 32 * ((grp * NK + k) // 2 + 1))
                    ve.tensor_mul(sg[:, k * GWD:(k + 1) * GWD],
                                  sy[:, pp * GWD:(pp + 1) * GWD],
                                  sz[:, pp * GWD:(pp + 1) * GWD])
                    K[("g", grp, k)] = ve.n
                    if k >= 2:
                        ve.wait(pe, K[("st", grp, k - 2)])
                    ve.tensor_mul(sg2[:, pp * GWD:(pp + 1) * GWD],
                                  sg[:, k * GWD:(k + 1) * GWD],
                                  sg[:, k * GWD:(k + 1) * GWD])
                    K[("g2", grp, k)] = ve.n

                @block.tensor
                def _(t, grp=grp, k=k, pp=pp):
                    pe.wait(ldc, ldc_done)
                    pe.wait(ve, K[("g2", grp, k)])
                    if grp > 0 and k == 0:
                        pe.wait(se, K[("rms", grp - 1)])
                    i = pe.matmul(pS[:, :], sones[:], sg2[:, pp * GWD:(pp + 1) * GWD],
                                  start=(k == 0), stop=(k == NK - 1))
                    K[("st", grp, k)] = pe.done(i)

            @block.vector
            def _(v, grp=grp):
                ve.wait(pe, K[("st", grp, NK - 1)])
                ve.tensor_scalar(srms[:, :], pS[:, :], 1.0 / D_INNER, LN_EPS,
                                 ALU.mult, ALU.add)
            K[("vrms", grp)] = ve.n

            @block.scalar
            def _(s, grp=grp):
                se.wait(ve, K[("vrms", grp)])
                se.activation(srms[:, :], srms[:, :], AF.Ln)
                se.activation(srms[:, :], srms[:, :], AF.Exp, scale=-0.5)
            K[("rms", grp)] = se.n

            for k in range(NK):
                @block.vector
                def _(v, grp=grp, k=k):
                    if k == 0:
                        ve.wait(se, K[("rms", grp)])
                    sl = sg[:, k * GWD:(k + 1) * GWD]
                    ve.tensor_scalar_mul(sl, sl, snw[:, k:k + 1])
                    ve.tensor_mul(sl, sl, srms[:, :])
                K[("gn", grp)] = ve.n

            for ot in range(NO):
                pp = ot % 2
                gi = grp * NO + ot

                @block.gpsimd
                def _(g, grp=grp, ot=ot, pp=pp, gi=gi):
                    if gi >= 2:
                        gq.wait(pe, K[("mm", gi - 2)])
                    gq.start(sw[:, pp * NK * 128:(pp + 1) * NK * 128],
                             woT[:, ot * 128:(ot + 1) * 128]
                             .rearrange("(k p) e -> p k e", p=128), sem=ldw[pp])
                    nwload[pp] += 16

                @block.sync
                def _(s, grp=grp, go=go, ot=ot, pp=pp, gi=gi):
                    if gi >= 2:
                        sq.wait(ve, K[("x1", gi - 2)])
                    sq.start(su[:, pp * GWD:(pp + 1) * GWD],
                             u_fm[ot * 128:(ot + 1) * 128, go:go + GWD], sem=ldu[pp])
                    nuload[pp] += 16

                @block.tensor
                def _(t, grp=grp, ot=ot, pp=pp, gi=gi):
                    pe.wait(ldw[pp], 16 * (gi // 2 + 1))
                    if ot == 0:
                        pe.wait(ve, K[("gn", grp)])
                    if gi >= 2:
                        pe.wait(ve, K[("x1", gi - 2)])
                    for k in range(NK):
                        i = pe.matmul(pmm[:, pp * 512:pp * 512 + GWD],
                                      sw[:, pp * NK * 128 + k * 128:pp * NK * 128 + (k + 1) * 128],
                                      sg[:, k * GWD:(k + 1) * GWD],
                                      start=(k == 0), stop=(k == NK - 1))
                    K[("mm", gi)] = pe.done(i)

                @block.vector
                def _(v, grp=grp, ot=ot, pp=pp, gi=gi):
                    ve.wait(pe, K[("mm", gi)])
                    ve.wait(ldu[pp], 16 * (gi // 2 + 1))
                    if gi >= 2:
                        ve.wait(stx[pp], 16 * (gi // 2))
                    ve.tensor_add(sx1[:, pp * GWD:(pp + 1) * GWD],
                                  pmm[:, pp * 512:pp * 512 + GWD],
                                  su[:, pp * GWD:(pp + 1) * GWD])
                K[("x1", gi)] = ve.n

                @block.sync
                def _(s, grp=grp, go=go, ot=ot, pp=pp, gi=gi):
                    sq.wait(ve, K[("x1", gi)])
                    sq.start(x1_o[ot * 128:(ot + 1) * 128, go:go + GWD],
                             sx1[:, pp * GWD:(pp + 1) * GWD], sem=stx[pp])
                    nxst[pp] += 16

        @block.sync
        def _(s):
            for sem in (stx[0], stx[1], ldc, ldu[0], ldu[1]):
                s.wait_ge(sem.h, sem.n)

        @block.gpsimd
        def _(g):
            for sem in (ldw[0], ldw[1]):
                g.wait_ge(sem.h, sem.n)

    return nc


def host_mgate(y_all, zs_all, u_flat, out_proj_w, norm_w):
    woT = _f32(out_proj_w.T)
    nw = _f32(norm_w.reshape(32, 128).T)
    ones_bf = np.ones((128, 128), dtype=BF)
    in_maps = []
    for c in range(NCORES):
        s0 = c * TC
        in_maps.append({
            "y_fm": _bf(np.asarray(y_all[c]).T),
            "zs": zs_all[c] if zs_all[c].dtype == BF else _bf(zs_all[c]),
            "u_fm": _f32(u_flat[s0:s0 + TC].T),
            "woT": woT,
            "normw": nw,
            "ones_bf": ones_bf,
        })
    return in_maps

EMOE = 64
CAPM = 1024
TOPK = 6
ROWS_E = CAPM + 1


def build_moe_route():
    nc = bass.Bass("TRN2", target_bir_lowering=False)
    x1 = nc.dram_tensor("x1", [D, TC], F32, kind="ExternalInput").ap()
    wg = nc.dram_tensor("wg", [D, EMOE], F32, kind="ExternalInput").ap()
    wsu = nc.dram_tensor("wsu", [D, 1024], F32, kind="ExternalInput").ap()
    wsd = nc.dram_tensor("wsd", [128, 64 * 128], F32, kind="ExternalInput").ap()
    iot = nc.dram_tensor("iot", [128, EMOE], F32, kind="ExternalInput").ap()
    trils = nc.dram_tensor("trils", [128, 128], F32, kind="ExternalInput").ap()
    onek1 = nc.dram_tensor("onek1", [1, 128], F32, kind="ExternalInput").ap()
    onecl = nc.dram_tensor("onecl", [128, 1], F32, kind="ExternalInput").ap()
    rankm = nc.dram_tensor("rankm", [8, 1], F32, kind="ExternalInput").ap()
    tokbase = nc.dram_tensor("tokbase", [128, 1], F32, kind="ExternalInput").ap()
    ident_bf = nc.dram_tensor("ident_bf", [128, 128], BF16, kind="ExternalInput").ap()
    ones_bf = nc.dram_tensor("ones_bf", [128, 128], BF16, kind="ExternalInput").ap()

    xnf_o = nc.dram_tensor("xnf", [D, TC], BF16, kind="ExternalOutput").ap()
    xnt_o = nc.dram_tensor("xnt", [TC, D], BF16, kind="ExternalOutput").ap()
    sht_o = nc.dram_tensor("sht", [TC, D], BF16, kind="ExternalOutput").ap()
    slot_o = nc.dram_tensor("slot", [TC, TOPK], F32, kind="ExternalOutput").ap()
    wv_o = nc.dram_tensor("wv", [TC, TOPK], F32, kind="ExternalOutput").ap()
    cnt_o = nc.dram_tensor("cnt", [1, EMOE], F32, kind="ExternalOutput").ap()
    tbl_o = nc.dram_tensor("tbl", [EMOE * ROWS_E, 1], F32, kind="ExternalOutput").ap()

    cc_cnt = nc.dram_tensor("cc_cnt", [1, EMOE], F32).ap()
    ag_cnt = nc.dram_tensor("ag_cnt", [NCORES, EMOE], F32).ap()

    NG = 4
    GWD = 512
    NT = 16

    with contextlib.ExitStack() as ctx:
        sx = ctx.enter_context(nc.sbuf_tensor("sx", [128, 16 * GWD], F32))
        sxn = ctx.enter_context(nc.sbuf_tensor("sxn", [128, 16 * TC], BF16))
        sx2 = ctx.enter_context(nc.sbuf_tensor("sx2", [128, 2 * GWD], BF16))
        sm = ctx.enter_context(nc.sbuf_tensor("smn", [128, GWD], F32))
        srs = ctx.enter_context(nc.sbuf_tensor("srs", [128, GWD], F32))
        svar = ctx.enter_context(nc.sbuf_tensor("svar", [128, GWD], F32))
        swg = ctx.enter_context(nc.sbuf_tensor("swg", [128, 16 * EMOE], BF16))
        swgf = ctx.enter_context(nc.sbuf_tensor("swgf", [128, 16 * EMOE], F32))
        swsu = ctx.enter_context(nc.sbuf_tensor("swsu", [128, 2 * 16 * 128], BF16))
        swsd2 = ctx.enter_context(nc.sbuf_tensor("swsd2", [128, 64 * 128], BF16))
        ssc = ctx.enter_context(nc.sbuf_tensor("ssc", [128, NT * EMOE], F32))
        smax = ctx.enter_context(nc.sbuf_tensor("smax", [128, 8], F32))
        srem = ctx.enter_context(nc.sbuf_tensor("srem", [128, EMOE], F32))
        smask = ctx.enter_context(nc.sbuf_tensor("smask", [128, NT * EMOE], F32))
        spos = ctx.enter_context(nc.sbuf_tensor("spos", [128, NT * EMOE], F32))
        siota = ctx.enter_context(nc.sbuf_tensor("siota", [128, EMOE], F32))
        stril = ctx.enter_context(nc.sbuf_tensor("strils", [128, 128], F32))
        sone1 = ctx.enter_context(nc.sbuf_tensor("sone1", [1, 128], F32))
        sonec = ctx.enter_context(nc.sbuf_tensor("sonecl", [128, 1], F32))
        srkm = ctx.enter_context(nc.sbuf_tensor("srkm", [8, 1], F32))
        stkb = ctx.enter_context(nc.sbuf_tensor("stkb", [128, 1], F32))
        soff = ctx.enter_context(nc.sbuf_tensor("soff", [1, EMOE], F32))
        sgoff = ctx.enter_context(nc.sbuf_tensor("sgoff", [128, EMOE], F32))
        szer = ctx.enter_context(nc.sbuf_tensor("szer", [128, EMOE], F32))
        scum = ctx.enter_context(nc.sbuf_tensor("scum", [128, EMOE], F32))
        soh = ctx.enter_context(nc.sbuf_tensor("soh", [128, EMOE], F32))
        sd1 = ctx.enter_context(nc.sbuf_tensor("sd1", [128, EMOE], F32))
        scol = ctx.enter_context(nc.sbuf_tensor("scol", [128, 16], F32))
        sslot = ctx.enter_context(nc.sbuf_tensor("sslot", [128, TOPK], F32))
        swv = ctx.enter_context(nc.sbuf_tensor("swv", [128, TOPK], F32))
        svald = ctx.enter_context(nc.sbuf_tensor("svald", [128, TOPK], F32))
        sslo32 = ctx.enter_context(nc.sbuf_tensor("sslo32", [128, 1], I32))
        stok = ctx.enter_context(nc.sbuf_tensor("stok", [128, 1], F32))
        sidn = ctx.enter_context(nc.sbuf_tensor("sidnb", [128, 128], BF16))
        sonesb = ctx.enter_context(nc.sbuf_tensor("sonesb", [128, 128], BF16))
        sonef = ctx.enter_context(nc.sbuf_tensor("sonef", [128, 128], F32))
        ssu = ctx.enter_context(nc.sbuf_tensor("ssu", [128, 8 * GWD], BF16))
        sshf = ctx.enter_context(nc.sbuf_tensor("sshf", [128, 16 * GWD], BF16))
        sstg = ctx.enter_context(nc.sbuf_tensor("sstg", [128, 2 * 256], BF16))

        pS1 = ctx.enter_context(nc.psum_tensor("pS1", [128, 512], F32))
        pS2 = ctx.enter_context(nc.psum_tensor("pS2", [128, 512], F32))
        psc = ctx.enter_context(nc.psum_tensor("psc", [128, 512], F32))
        ppos = ctx.enter_context(nc.psum_tensor("ppos", [128, 512], F32))
        pmm = ctx.enter_context(nc.psum_tensor("pmm", [128, 2 * 512], F32))
        ptr = ctx.enter_context(nc.psum_tensor("ptr", [128, 512], BF16))

        block = ctx.enter_context(nc.Block())

        class NS:
            pass
        E = NS()

        @block.vector
        def _(v):
            E.ve = Eng(ctx, nc, v, "ve")

        @block.scalar
        def _(s):
            E.se = Eng(ctx, nc, s, "se")

        @block.tensor
        def _(t):
            E.pe = Pe(ctx, nc)

        @block.gpsimd
        def _(g):
            E.gq = Dma(ctx, nc, g, "gq")

        @block.sync
        def _(s):
            E.sq = Dma(ctx, nc, s, "sq")

        ve, se, pe, gq, sq = E.ve, E.se, E.pe, E.gq, E.sq
        ldc = Sem(ctx, nc, "ldc")
        ldx = Sem(ctx, nc, "ldx")
        ldsu = [Sem(ctx, nc, "ldsu0"), Sem(ctx, nc, "ldsu1")]
        stf = Sem(ctx, nc, "stf")
        agc = Sem(ctx, nc, "agc")
        agl = Sem(ctx, nc, "agl")
        sct = Sem(ctx, nc, "sct")
        stsw = Sem(ctx, nc, "stsw")
        stt_ = [Sem(ctx, nc, "stt0"), Sem(ctx, nc, "stt1")]

        @block.sync
        def _(s):
            sq.start(swgf[:, :].rearrange("p (k e) -> p k e", k=16),
                     wg.rearrange("(k p) e -> p k e", p=128), sem=ldc)
            sq.start(siota[:, :], iot, sem=ldc)
            sq.start(stril[:, :], trils, sem=ldc)
            sq.start(sone1[:, :], onek1, sem=ldc)
            sq.start(sonec[:, :], onecl, sem=ldc)
            sq.start(srkm[:, :], rankm, sem=ldc)
            sq.start(stkb[:, :], tokbase, sem=ldc)
            sq.start(sidn[:, :], ident_bf, sem=ldc)
            sq.start(sonesb[:, :], ones_bf, sem=ldc)
            pass
        ldc_done = ldc.n

        @block.gpsimd
        def _(g):
            gq.start(swsd2[:, :], wsd, sem=ldc)

        @block.vector
        def _(v):
            ve.wait(ldc, ldc_done)
            ve.tensor_copy(swg[:, :], swgf[:, :])
            ve.memset(szer[:, :], 0.0)
            ve.memset(soff[:, :], 0.0)
            ve.memset(sonef[:, :], 1.0)
        v_cst = ve.n

        K = {}
        for g in range(NG):
            go = g * GWD

            @block.gpsimd
            def _(gg, g=g, go=go):
                if g >= 1:
                    gq.wait(ve, K[("xn", g - 1)])
                for k in range(16):
                    gq.start(sx[:, k * GWD:(k + 1) * GWD],
                             x1[k * 128:(k + 1) * 128, go:go + GWD], sem=ldx)
            K[("ldx", g)] = ldx.n

            for k in range(16):
                pp = k % 2

                @block.vector
                def _(v, g=g, k=k, pp=pp):
                    if k == 0:
                        ve.wait(ldx, K[("ldx", g)])
                    if k >= 2:
                        ve.wait(pe, K[("st2", g, k - 2)])
                    ve.tensor_mul(sx2[:, pp * GWD:(pp + 1) * GWD],
                                  sx[:, k * GWD:(k + 1) * GWD],
                                  sx[:, k * GWD:(k + 1) * GWD])
                    K[("x2", g, k)] = ve.n

                @block.tensor
                def _(t, g=g, k=k, pp=pp):
                    if g > 0 and k == 0:
                        pe.wait(se, K[("rs", g - 1)])
                    pe.wait(ve, K[("x2", g, k)])
                    if k == 0:
                        pe.wait(ve, v_cst)
                    pe.matmul(pS1[:, :], sonef[:], sx[:, k * GWD:(k + 1) * GWD],
                              start=(k == 0), stop=(k == 15))
                    i = pe.matmul(pS2[:, :], sonesb[:], sx2[:, pp * GWD:(pp + 1) * GWD],
                                  start=(k == 0), stop=(k == 15))
                    K[("st2", g, k)] = pe.done(i)
                    if k == 15:
                        K[("stats", g)] = pe.n

            @block.vector
            def _(v, g=g):
                ve.wait(pe, K[("stats", g)])
                ve.tensor_scalar_mul(sm[:, :], pS1[:, :], 1.0 / D)
                ve.tensor_mul(svar[:, :], sm[:, :], sm[:, :])
                ve.scalar_tensor_tensor(svar[:, :], pS2[:, :], 1.0 / D, svar[:, :],
                                        ALU.mult, ALU.subtract)
                ve.tensor_scalar_add(svar[:, :], svar[:, :], LN_EPS)
            K[("var", g)] = ve.n

            @block.scalar
            def _(s, g=g):
                se.wait(ve, K[("var", g)])
                se.activation(srs[:, :], svar[:, :], AF.Ln)
                se.activation(srs[:, :], srs[:, :], AF.Exp, scale=-0.5)
            K[("rs", g)] = se.n

            @block.vector
            def _(v, g=g, go=go):
                ve.wait(se, K[("rs", g)])
                for k in range(16):
                    sl = sxn[:, k * TC + go:k * TC + go + GWD]
                    ve.tensor_sub(sx[:, k * GWD:(k + 1) * GWD], sx[:, k * GWD:(k + 1) * GWD],
                                  sm[:, :])
                    ve.tensor_mul(sl, sx[:, k * GWD:(k + 1) * GWD], srs[:, :])
            K[("xn", g)] = ve.n

        @block.sync
        def _(s):
            sq.wait(ve, K[("xn", NG - 1)])
            for k in range(16):
                sq.start(xnf_o[k * 128:(k + 1) * 128, :], sxn[:, k * TC:(k + 1) * TC], sem=stf)
        K["xnf_done"] = stf.n

        for tt in range(NT):
            @block.tensor
            def _(t, tt=tt):
                pe.wait(ve, K[("xn", NG - 1)])
                if tt >= 1:
                    pe.wait(se, K[("sig", tt - 1)])
                for k in range(16):
                    i = pe.matmul(psc[:, 0:EMOE], sxn[:, k * TC + tt * 128:k * TC + (tt + 1) * 128],
                                  swg[:, k * EMOE:(k + 1) * EMOE],
                                  start=(k == 0), stop=(k == 15))
                K[("sc", tt)] = pe.done(i)

            @block.scalar
            def _(s, tt=tt):
                se.wait(pe, K[("sc", tt)])
                se.activation(ssc[:, tt * EMOE:(tt + 1) * EMOE], psc[:, 0:EMOE], AF.Sigmoid)
            K[("sig", tt)] = se.n

        for tt in range(NT):
            @block.vector
            def _(v, tt=tt):
                ve.wait(se, K[("sig", tt)])
                ve.wait(ldc, ldc_done)
                sc = ssc[:, tt * EMOE:(tt + 1) * EMOE]
                ve.max(out=smax[:, :], in_=sc)
                ve.memset(smax[:, 6:8], -1e30)
                ve.match_replace(out=srem[:, :], in_to_replace=smax[:, :], in_values=sc,
                                 imm_value=-1e30)
                ve.tensor_tensor(smask[:, tt * EMOE:(tt + 1) * EMOE], sc, srem[:, :],
                                 ALU.not_equal)
            K[("msk", tt)] = ve.n

            @block.tensor
            def _(t, tt=tt):
                pe.wait(ve, K[("msk", tt)])
                if tt > 0:
                    pe.wait(ve, K[("pos", tt - 1)])
                pe.matmul(ppos[:, 0:EMOE], stril[:, :], smask[:, tt * EMOE:(tt + 1) * EMOE],
                          start=True, stop=False)
                i = pe.matmul(ppos[:, 0:EMOE], sone1[:, :], soff[:, :],
                              start=False, stop=True)
                K[("posm", tt)] = pe.done(i)
                i = pe.matmul(ppos[0:1, 256:256 + EMOE], sonec[:, :],
                              smask[:, tt * EMOE:(tt + 1) * EMOE], start=True, stop=True)
                K[("cnt", tt)] = pe.done(i)

            @block.vector
            def _(v, tt=tt):
                ve.wait(pe, K[("cnt", tt)])
                ve.tensor_copy(spos[:, tt * EMOE:(tt + 1) * EMOE], ppos[:, 0:EMOE])
                ve.tensor_add(soff[:, :], soff[:, :], ppos[0:1, 256:256 + EMOE])
            K[("pos", tt)] = ve.n

        @block.sync
        def _(s):
            sq.wait(ve, K[("pos", NT - 1)])
            sq.start(cnt_o, soff[:, :], sem=stf)
            sq.start(cc_cnt, soff[:, :], sem=sct)

        @block.gpsimd
        def _(g):
            g.wait_ge(sct.h, sct.n)
            gq.collective("AllGather", ALU.bypass, [list(range(NCORES))],
                          [cc_cnt.opt()], [ag_cnt.opt()], sem=agc)

        @block.sync
        def _(s):
            s.wait_ge(agc.h, agc.n)
            sq.start(srem[0:8, :], ag_cnt, sem=agl)
        K["agl"] = agl.n

        @block.tensor
        def _(t):
            pe.wait(agl, K["agl"])
            i = pe.matmul(ppos[0:1, 0:EMOE], srkm[:, :], srem[0:8, :], start=True, stop=True)
            pe.done(i)
            K["goffmm"] = pe.n

        @block.vector
        def _(v):
            ve.wait(pe, K["goffmm"])
            ve.tensor_copy(soff[:, :], ppos[0:1, 0:EMOE])
        K["goffrow"] = ve.n

        @block.tensor
        def _(t):
            pe.wait(ve, K["goffrow"])
            i = pe.matmul(ppos[:, 256:256 + EMOE], sone1[:, :], soff[:, :], start=True, stop=True)
            K["goffbc"] = pe.done(i)

        @block.vector
        def _(v):
            ve.wait(pe, K["goffbc"])
            ve.tensor_copy(sgoff[:, :], ppos[:, 256:256 + EMOE])
        K["goff"] = ve.n

        for tt in range(NT):
            @block.vector
            def _(v, tt=tt):
                if tt > 0:
                    ve.wait(sct, K[("sctt", tt - 1, TOPK - 1)])
                    ve.wait(stsw, 32 * tt)
                msk = smask[:, tt * EMOE:(tt + 1) * EMOE]
                sc = ssc[:, tt * EMOE:(tt + 1) * EMOE]
                ve.tensor_tensor_scan(scum[:, :], msk, szer[:, :], 0.0, ALU.add, ALU.add)
                ve.tensor_add(sd1[:, :], spos[:, tt * EMOE:(tt + 1) * EMOE], sgoff[:, :])
                for k in range(TOPK):
                    ve.tensor_scalar(soh[:, :], scum[:, :], float(k + 1), None, ALU.is_equal)
                    ve.tensor_mul(soh[:, :], soh[:, :], msk)
                    ve.tensor_mul(srem[:, :], soh[:, :], siota[:, :])
                    ve.tensor_reduce(scol[:, 0:1], srem[:, :], mybir.AxisListType.X, ALU.add)
                    ve.tensor_mul(srem[:, :], soh[:, :], sd1[:, :])
                    ve.tensor_reduce(scol[:, 1:2], srem[:, :], mybir.AxisListType.X, ALU.add)
                    ve.tensor_mul(srem[:, :], soh[:, :], sc)
                    ve.tensor_reduce(swv[:, k:k + 1], srem[:, :], mybir.AxisListType.X, ALU.add)
                    ve.tensor_scalar(svald[:, k:k + 1], scol[:, 1:2], 1024.0, None, ALU.is_lt)
                    ve.tensor_scalar_min(scol[:, 1:2], scol[:, 1:2], 1024.0)
                    ve.scalar_tensor_tensor(sslot[:, k:k + 1], scol[:, 0:1], 1025.0,
                                            scol[:, 1:2], ALU.mult, ALU.add)
                ve.tensor_reduce(scol[:, 2:3], swv[:, :], mybir.AxisListType.X, ALU.add)
                ve.reciprocal(scol[:, 3:4], scol[:, 2:3])
                ve.tensor_scalar_mul(swv[:, :], swv[:, :], scol[:, 3:4])
                ve.tensor_mul(swv[:, :], swv[:, :], svald[:, :])
                ve.tensor_scalar_add(stok[:, :], stkb[:, :], float(tt * 128))
            K[("ext", tt)] = ve.n

            @block.sync
            def _(s, tt=tt):
                sq.wait(ve, K[("ext", tt)])
                sq.start(slot_o[tt * 128:(tt + 1) * 128, :], sslot[:, :], sem=stsw)
                sq.start(wv_o[tt * 128:(tt + 1) * 128, :], swv[:, :], sem=stsw)
            K[("exts", tt)] = stsw.n

            for k in range(TOPK):
                @block.vector
                def _(v, tt=tt, k=k):
                    if (tt, k) != (0, 0):
                        gq_prev = K.get(("sctt", tt, k - 1) if k else ("sctt", tt - 1, TOPK - 1))
                        ve.wait(sct, gq_prev)
                    ve.tensor_copy(sslo32[:, :], sslot[:, k:k + 1])
                K[("i32", tt, k)] = ve.n

                @block.gpsimd
                def _(g, tt=tt, k=k):
                    gq.wait(ve, K[("i32", tt, k)])
                    gq.indirect(out=tbl_o, out_offset=IndirectOffsetOnAxis(sslo32[:, :], axis=0),
                                in_=stok[:, :], in_offset=None, sem=sct)
                K[("sctt", tt, k)] = sct.n

        for grp in range(NG):
            go = grp * GWD
            for mt in range(8):
                pp = mt % 2
                gi = grp * 8 + mt

                @block.gpsimd
                def _(g, grp=grp, mt=mt, pp=pp, gi=gi):
                    if gi >= 2:
                        gq.wait(pe, K[("su", gi - 2)])
                    gq.start(swsu[:, pp * 16 * 128:(pp + 1) * 16 * 128],
                             wsu[:, mt * 128:(mt + 1) * 128].rearrange("(k p) m -> p k m", p=128),
                             sem=ldsu[pp])
                K[("ldsu", gi)] = 16 * (gi // 2 + 1)

                @block.tensor
                def _(t, grp=grp, go=go, mt=mt, pp=pp, gi=gi):
                    pe.wait(ldsu[pp], K[("ldsu", gi)])
                    if gi >= 2:
                        pe.wait(ve, K[("hh", gi - 2)] if (mt % 2 == 0) else K[("hh", gi - 2)])
                    for k in range(16):
                        i = pe.matmul(pmm[:, pp * 512:pp * 512 + GWD],
                                      swsu[:, pp * 16 * 128 + k * 128:pp * 16 * 128 + (k + 1) * 128],
                                      sxn[:, k * TC + go:k * TC + go + GWD],
                                      start=(k == 0), stop=(k == 15))
                    K[("su", gi)] = pe.done(i)

                @block.scalar
                def _(s, grp=grp, mt=mt, pp=pp, gi=gi):
                    se.wait(pe, K[("su", gi)])
                    if mt < 4:
                        se.copy(ssu[:, mt * GWD:(mt + 1) * GWD], pmm[:, pp * 512:pp * 512 + GWD])
                    else:
                        se.activation(ssu[:, (mt % 4 + 4) * GWD:(mt % 4 + 5) * GWD],
                                      pmm[:, pp * 512:pp * 512 + GWD], AF.Silu)
                K[("hh", gi)] = se.n

            @block.vector
            def _(v, grp=grp):
                ve.wait(se, K[("hh", grp * 8 + 7)])
                for j in range(4):
                    ve.tensor_mul(ssu[:, j * GWD:(j + 1) * GWD],
                                  ssu[:, j * GWD:(j + 1) * GWD],
                                  ssu[:, (j + 4) * GWD:(j + 5) * GWD])
            K[("hmul", grp)] = ve.n

            for ot in range(16):
                @block.tensor
                def _(t, grp=grp, ot=ot):
                    pe.wait(ve, K[("hmul", grp)])
                    if ot >= 1:
                        pe.wait(se, K[("shev", grp, ot - 1)])
                    for k in range(4):
                        i = pe.matmul(pmm[:, 0:GWD],
                                      swsd2[:, (k * 16 + ot) * 128:(k * 16 + ot + 1) * 128],
                                      ssu[:, k * GWD:(k + 1) * GWD],
                                      start=(k == 0), stop=(k == 3))
                    K[("shmm", grp, ot)] = pe.done(i)

                @block.scalar
                def _(s, grp=grp, ot=ot):
                    se.wait(pe, K[("shmm", grp, ot)])
                    se.copy(sshf[:, ot * GWD:(ot + 1) * GWD], pmm[:, 0:GWD])
                K[("shev", grp, ot)] = se.n

            for sub in range(4):
                tt = grp * 4 + sub
                for dt_ in range(16):
                    idx = (tt * 16 + dt_)
                    par = idx % 2

                    @block.tensor
                    def _(t, grp=grp, go=go, sub=sub, tt=tt, dt_=dt_, par=par, idx=idx):
                        if dt_ == 0 and sub == 0:
                            pe.wait(se, K[("shev", grp, 15)])
                        prev = K.get(("trev", idx - 1))
                        if prev:
                            pe.wait(ve, prev)
                        i = pe.transpose(ptr[:, 0:128], sxn[:, dt_ * TC + tt * 128:dt_ * TC + (tt + 1) * 128], sidn[:, :])
                        pe.done(i)
                        i = pe.transpose(ptr[:, 128:256], sshf[:, dt_ * GWD + sub * 128:dt_ * GWD + (sub + 1) * 128], sidn[:, :])
                        K[("tr", idx)] = pe.done(i)

                    @block.vector
                    def _(v, tt=tt, dt_=dt_, par=par, idx=idx):
                        ve.wait(pe, K[("tr", idx)])
                        if idx >= 2:
                            ve.wait(stt_[par], 32 * (idx // 2))
                        ve.tensor_copy(sstg[:, par * 256:par * 256 + 128], ptr[:, 0:128])
                        ve.tensor_copy(sstg[:, par * 256 + 128:par * 256 + 256], ptr[:, 128:256])
                    K[("trev", idx)] = ve.n

                    @block.sync
                    def _(s, tt=tt, dt_=dt_, par=par, idx=idx):
                        sq.wait(ve, K[("trev", idx)])
                        sq.start(xnt_o[tt * 128:(tt + 1) * 128, dt_ * 128:(dt_ + 1) * 128],
                                 sstg[:, par * 256:par * 256 + 128], sem=stt_[par])
                        sq.start(sht_o[tt * 128:(tt + 1) * 128, dt_ * 128:(dt_ + 1) * 128],
                                 sstg[:, par * 256 + 128:par * 256 + 256], sem=stt_[par])

        @block.sync
        def _(s):
            for sem in (stf, stt_[0], stt_[1], ldc, agl):
                s.wait_ge(sem.h, sem.n)

        @block.gpsimd
        def _(g):
            for sem in (sct, agc, ldx, ldsu[0], ldsu[1]):
                g.wait_ge(sem.h, sem.n)

    return nc


def host_moe_route(x1_all, w_gate, w_shared_up, w_shared_down):
    wsd_t = np.zeros((128, 64 * 128), np.float32)
    for k in range(4):
        for mt in range(16):
            wsd_t[:, (k * 16 + mt) * 128:(k * 16 + mt + 1) * 128] = \
                w_shared_down[k * 128:(k + 1) * 128, mt * 128:(mt + 1) * 128]
    iot = np.broadcast_to(np.arange(EMOE, dtype=np.float32), (128, EMOE)).copy()
    trils = (np.arange(128)[None, :] < np.arange(128)[:, None]).astype(np.float32)
    onek1 = np.ones((1, 128), np.float32)
    onecl = np.ones((128, 1), np.float32)
    ident_bf = np.eye(128, dtype=np.float32).astype(BF)
    ones_bf = np.ones((128, 128), dtype=BF)
    in_maps = []
    for c in range(NCORES):
        rk = np.zeros((8, 1), np.float32)
        rk[:c] = 1.0
        tb = (np.arange(128, dtype=np.float32) + c * TC).reshape(128, 1)
        in_maps.append({
            "x1": _f32(x1_all[c]),
            "wg": _f32(w_gate),
            "wsu": _f32(w_shared_up),
            "wsd": wsd_t,
            "iot": iot, "trils": trils, "onek1": onek1, "onecl": onecl,
            "rankm": rk, "tokbase": tb,
            "ident_bf": ident_bf, "ones_bf": ones_bf,
        })
    return in_maps



def build_moe_expert():
    nc = bass.Bass("TRN2", target_bir_lowering=False)
    xg = nc.dram_tensor("xg", [NTOK, D], BF16, kind="ExternalInput").ap()
    tbls = nc.dram_tensor("tbls", [NCORES, 8 * ROWS_E], F32, kind="ExternalInput").ap()
    cntT = nc.dram_tensor("cntT", [8, 8], F32, kind="ExternalInput").ap()
    wup = nc.dram_tensor("wup", [8 * D, 512], F32, kind="ExternalInput").ap()
    wdn = nc.dram_tensor("wdn", [8 * 256, D], F32, kind="ExternalInput").ap()
    islot = nc.dram_tensor("islot", [8, ROWS_E], F32, kind="ExternalInput").ap()
    idn = nc.dram_tensor("idn", [128, 128], F32, kind="ExternalInput").ap()

    y_o = nc.dram_tensor("Yout", [8 * ROWS_E, D], BF16, kind="ExternalOutput").ap()
    xe_d = nc.dram_tensor("xe_d", [8 * CAPM, D], BF16).ap()
    ye_d = nc.dram_tensor("ye_d", [D, CAPM], BF16).ap()

    with contextlib.ExitStack() as ctx:
        stbl = ctx.enter_context(nc.sbuf_tensor("stbl", [8, ROWS_E], F32))
        smrg = ctx.enter_context(nc.sbuf_tensor("smrg", [8, ROWS_E], F32))
        scnt = ctx.enter_context(nc.sbuf_tensor("scnt", [8, 8], F32))
        sgo = ctx.enter_context(nc.sbuf_tensor("sgo", [8, 8], F32))
        send = ctx.enter_context(nc.sbuf_tensor("send", [8, 8], F32))
        szer8 = ctx.enter_context(nc.sbuf_tensor("szer8", [8, 8], F32))
        sio = ctx.enter_context(nc.sbuf_tensor("sio", [8, ROWS_E], F32))
        si1 = ctx.enter_context(nc.sbuf_tensor("si1", [8, ROWS_E], F32))
        si2 = ctx.enter_context(nc.sbuf_tensor("si2", [8, ROWS_E], F32))
        sidn = ctx.enter_context(nc.sbuf_tensor("sidn", [128, 128], F32))
        stc = ctx.enter_context(nc.sbuf_tensor("stc", [128, 64], F32))
        sti = ctx.enter_context(nc.sbuf_tensor("sti", [128, 64], I32))
        sxe = ctx.enter_context(nc.sbuf_tensor("sxe", [128, 2 * 16 * 1024], BF16))
        swu = ctx.enter_context(nc.sbuf_tensor("swu", [128, 2 * 64 * 128], BF16))
        swd = ctx.enter_context(nc.sbuf_tensor("swd", [128, 2 * 32 * 128], BF16))
        sh = ctx.enter_context(nc.sbuf_tensor("sh", [128, 4 * 512], BF16))
        sg_ = ctx.enter_context(nc.sbuf_tensor("sg_", [128, 1024], BF16))
        syst = ctx.enter_context(nc.sbuf_tensor("syst", [128, 2 * 512], BF16))
        sytm = ctx.enter_context(nc.sbuf_tensor("sytm", [128, 2 * D], BF16))
        sgst = ctx.enter_context(nc.sbuf_tensor("sgst", [128, 2 * D], BF16))
        szero = ctx.enter_context(nc.sbuf_tensor("szero", [1, D], BF16))

        pmm = ctx.enter_context(nc.psum_tensor("pmm", [128, 4 * 512], F32))
        ptr = ctx.enter_context(nc.psum_tensor("ptr", [128, 512], F32))

        block = ctx.enter_context(nc.Block())

        class NS:
            pass
        E = NS()

        @block.vector
        def _(v):
            E.ve = Eng(ctx, nc, v, "ve")

        @block.scalar
        def _(s):
            E.se = Eng(ctx, nc, s, "se")

        @block.tensor
        def _(t):
            E.pe = Pe(ctx, nc)

        @block.gpsimd
        def _(g):
            E.gq = Dma(ctx, nc, g, "gq")

        @block.sync
        def _(s):
            E.sq = Dma(ctx, nc, s, "sq")

        ve, se, pe, gq, sq = E.ve, E.se, E.pe, E.gq, E.sq
        ldc = Sem(ctx, nc, "ldc")
        ldt = Sem(ctx, nc, "ldt")
        gat = Sem(ctx, nc, "gat")
        ldxe = [Sem(ctx, nc, "ldxe0"), Sem(ctx, nc, "ldxe1")]
        ldwu = [Sem(ctx, nc, "ldwu0"), Sem(ctx, nc, "ldwu1")]
        ldwd = [Sem(ctx, nc, "ldwd0"), Sem(ctx, nc, "ldwd1")]
        stye = Sem(ctx, nc, "stye")
        ldyt = Sem(ctx, nc, "ldyt")
        sty = [Sem(ctx, nc, "sty0"), Sem(ctx, nc, "sty1")]
        sttr = Sem(ctx, nc, "sttr")

        @block.sync
        def _(s):
            sq.start(scnt[:, :], cntT, sem=ldc)
            sq.start(sio[:, :], islot, sem=ldc)
            sq.start(sidn[:, :], idn, sem=ldc)
        ldc_done = ldc.n

        @block.vector
        def _(v):
            ve.wait(ldc, ldc_done)
            ve.memset(szer8[:, :], 0.0)
            ve.memset(szero[:, :], 0.0)
            ve.tensor_tensor_scan(sgo[:, :], scnt[:, :], szer8[:, :], 0.0, ALU.add, ALU.add)
            ve.tensor_copy(send[:, 1:8], sgo[:, 0:7])
            ve.memset(send[:, 0:1], 0.0)
            ve.tensor_add(sgo[:, :], send[:, :], scnt[:, :])
            ve.memset(smrg[:, :], 0.0)
        v_go = ve.n

        K = {}
        for r in range(NCORES):
            @block.sync
            def _(s, r=r):
                if r == 0:
                    sq.wait(ve, v_go)
                else:
                    sq.wait(ve, K[("mrg", r - 1)])
                sq.start(stbl[:, :], tbls[r:r + 1, :].rearrange("o (e s) -> (o e) s", e=8),
                         sem=ldt)
            ldt_r = ldt.n

            @block.vector
            def _(v, r=r, ldt_r=ldt_r):
                ve.wait(ldt, ldt_r)
                ve.tensor_scalar(si1[:, :], sio[:, :], send[:, r:r + 1], None, ALU.is_ge)
                ve.tensor_scalar(si2[:, :], sio[:, :], sgo[:, r:r + 1], None, ALU.is_lt)
                ve.tensor_mul(si1[:, :], si1[:, :], si2[:, :])
                ve.tensor_mul(si1[:, :], si1[:, :], stbl[:, :])
                ve.tensor_add(smrg[:, :], smrg[:, :], si1[:, :])
            K[("mrg", r)] = ve.n

        for st in range(8):
            @block.tensor
            def _(t, st=st):
                if st == 0:
                    pe.wait(ve, K[("mrg", NCORES - 1)])
                else:
                    pe.wait(ve, K[("trc", st - 1)])
                i = pe.transpose(ptr[:, 0:8], smrg[:, st * 128:(st + 1) * 128], sidn[0:8, 0:8])
                K[("tr", st)] = pe.done(i)

            @block.vector
            def _(v, st=st):
                ve.wait(pe, K[("tr", st)])
                ve.tensor_copy(stc[:, st * 8:(st + 1) * 8], ptr[:, 0:8])
            K[("trc", st)] = ve.n

        @block.vector
        def _(v):
            ve.tensor_copy(sti[:, :], stc[:, :])
        v_idx = ve.n

        gstg = [Sem(ctx, nc, "gstg0"), Sem(ctx, nc, "gstg1")]
        gi = 0
        for el in range(8):
            for st in range(8):
                par = gi % 2

                @block.gpsimd
                def _(g, el=el, st=st, par=par, gi=gi):
                    if gi == 0:
                        gq.wait(ve, v_idx)
                    if gi >= 2:
                        g.wait_ge(gstg[par].h, 16 * (gi // 2))
                    gq.indirect(out=sgst[:, par * D:(par + 1) * D],
                                out_offset=None, in_=xg,
                                in_offset=IndirectOffsetOnAxis(sti[:, st * 8 + el:st * 8 + el + 1], axis=0),
                                sem=gat)
                K[("gath", gi)] = gat.n

                @block.sync
                def _(s, el=el, st=st, par=par, gi=gi):
                    s.wait_ge(gat.h, K[("gath", gi)])
                    sq.start(xe_d[el * CAPM + st * 128:el * CAPM + (st + 1) * 128, :],
                             sgst[:, par * D:(par + 1) * D], sem=gstg[par])
                gi += 1
        gat_all = (gstg[0].n, gstg[1].n)

        for el in range(8):
            pp = el % 2

            @block.gpsimd
            def _(g, el=el, pp=pp):
                if el >= 2:
                    gq.wait(pe, K[("upmm", el - 2)])
                gq.start(swu[:, pp * 8192:(pp + 1) * 8192],
                         wup[el * D:(el + 1) * D, :]
                         .rearrange("(k p) (j m) -> p j k m", p=128, m=128), sem=ldwu[pp])
                if el >= 2:
                    gq.wait(pe, K[("dnmm", el - 2)])
                gq.start(swd[:, pp * 4096:(pp + 1) * 4096],
                         wdn[el * 256:(el + 1) * 256, :]
                         .rearrange("(k p) (t m) -> p t k m", p=128, m=128), sem=ldwd[pp])
            K[("ldw", el)] = (ldwu[pp].n, ldwd[pp].n)

            @block.sync
            def _(s, el=el, pp=pp):
                if el == 0:
                    s.wait_ge(gstg[0].h, gat_all[0])
                    s.wait_ge(gstg[1].h, gat_all[1])
                if el >= 2:
                    sq.wait(pe, K[("upmm", el - 2)])
                for dt_ in range(16):
                    sq.transpose(sxe[:, pp * 16384 + dt_ * 1024:pp * 16384 + (dt_ + 1) * 1024],
                                 xe_d[el * CAPM:(el + 1) * CAPM, dt_ * 128:(dt_ + 1) * 128],
                                 sem=ldxe[pp])
            K[("ldxe", el)] = ldxe[pp].n

            for n2 in range(2):
                @block.tensor
                def _(t, el=el, pp=pp, n2=n2):
                    if n2 == 0:
                        pe.wait(ldxe[pp], K[("ldxe", el)])
                        pe.wait(ldwu[pp], K[("ldw", el)][0])
                        if el >= 1:
                            pe.wait(ve, K[("h", el - 1, 1)])
                            pe.wait(se, K[("ydev", el - 1, 15)])
                    else:
                        pe.wait(ve, K[("h", el, 0)])
                    for jt in range(4):
                        for k in range(16):
                            i = pe.matmul(pmm[:, jt * 512:(jt + 1) * 512],
                                          swu[:, pp * 8192 + (jt * 16 + k) * 128:
                                              pp * 8192 + (jt * 16 + k + 1) * 128],
                                          sxe[:, pp * 16384 + k * 1024 + n2 * 512:
                                              pp * 16384 + k * 1024 + (n2 + 1) * 512],
                                          start=(k == 0), stop=(k == 15))
                    K[("upmm", el, n2)] = pe.done(i)
                    if n2 == 1:
                        K[("upmm", el)] = pe.n

                @block.scalar
                def _(s, el=el, pp=pp, n2=n2):
                    se.wait(pe, K[("upmm", el, n2)])
                    se.activation(sg_[:, 0:512], pmm[:, 0:512], AF.Silu)
                    se.activation(sg_[:, 512:1024], pmm[:, 512:1024], AF.Silu)
                K[("silu", el, n2)] = se.n

                @block.vector
                def _(v, el=el, pp=pp, n2=n2):
                    ve.wait(se, K[("silu", el, n2)])
                    ve.tensor_mul(sh[:, n2 * 512:(n2 + 1) * 512], sg_[:, 0:512],
                                  pmm[:, 1024:1536])
                    ve.tensor_mul(sh[:, 1024 + n2 * 512:1024 + (n2 + 1) * 512],
                                  sg_[:, 512:1024], pmm[:, 1536:2048])
                K[("h", el, n2)] = ve.n

            for mt in range(16):
                spp = mt % 2

                @block.tensor
                def _(t, el=el, pp=pp, mt=mt, spp=spp):
                    if mt == 0:
                        pe.wait(ve, K[("h", el, 1)])
                        pe.wait(ldwd[pp], K[("ldw", el)][1])
                    if mt >= 2:
                        pe.wait(se, K[("ydev", el, mt - 2)])
                    for n2 in range(2):
                        for kh in range(2):
                            i = pe.matmul(pmm[:, (spp * 2 + n2) * 512:(spp * 2 + n2 + 1) * 512],
                                          swd[:, pp * 4096 + (mt * 2 + kh) * 128:
                                              pp * 4096 + (mt * 2 + kh + 1) * 128],
                                          sh[:, kh * 1024 + n2 * 512:kh * 1024 + (n2 + 1) * 512],
                                          start=(kh == 0), stop=(kh == 1))
                    K[("dnmm", el, mt)] = pe.done(i)

                @block.scalar
                def _(s, el=el, mt=mt, spp=spp):
                    se.wait(pe, K[("dnmm", el, mt)])
                    if (el, mt) != (0, 0):
                        pel, pmt = (el, mt - 1) if mt else (el - 1, 15)
                        se.wait(stye, K[("yst", pel, pmt)])
                    se.copy(syst[:, 0:512], pmm[:, spp * 1024:spp * 1024 + 512])
                    se.copy(syst[:, 512:1024], pmm[:, spp * 1024 + 512:spp * 1024 + 1024])
                K[("ydev", el, mt)] = se.n

                @block.sync
                def _(s, el=el, mt=mt):
                    sq.wait(se, K[("ydev", el, mt)])
                    if el >= 1 and mt == 0:
                        s.wait_ge(ldyt.h, K[("ytr", el - 1, 7)])
                    sq.start(ye_d[mt * 128:(mt + 1) * 128, 0:512], syst[:, 0:512], sem=stye)
                    sq.start(ye_d[mt * 128:(mt + 1) * 128, 512:1024], syst[:, 512:1024], sem=stye)
                K[("yst", el, mt)] = stye.n
            K[("dnmm", el)] = pe.n

            for st in range(8):
                tpp = st % 2

                @block.sync
                def _(s, el=el, st=st, tpp=tpp):
                    if st == 0:
                        s.wait_ge(stye.h, K[("yst", el, 15)])
                    gi = el * 8 + st
                    if gi >= 2:
                        s.wait_ge(sty[tpp].h, 16 * (gi // 2))
                    sq.transpose(sytm[:, tpp * D:(tpp + 1) * D],
                                 ye_d[:, st * 128:(st + 1) * 128], sem=ldyt)
                K[("ytr", el, st)] = ldyt.n

                @block.sync
                def _(s, el=el, st=st, tpp=tpp):
                    s.wait_ge(ldyt.h, K[("ytr", el, st)])
                    sq.start(y_o[el * ROWS_E + st * 128:el * ROWS_E + (st + 1) * 128, :],
                             sytm[:, tpp * D:(tpp + 1) * D], sem=sty[tpp])

            @block.sync
            def _(s, el=el):
                sq.wait(ve, v_go)
                sq.start(y_o[el * ROWS_E + CAPM:el * ROWS_E + CAPM + 1, :], szero[:, :],
                         sem=sttr)

        @block.sync
        def _(s):
            for sem in (sty[0], sty[1], sttr, stye, ldt, ldc, ldyt, ldxe[0], ldxe[1]):
                s.wait_ge(sem.h, sem.n)

        @block.gpsimd
        def _(g):
            for sem in (gat, ldwu[0], ldwu[1], ldwd[0], ldwd[1], gstg[0], gstg[1]):
                g.wait_ge(sem.h, sem.n)

    return nc


def host_moe_expert(xnt_all, tbl_all, cnt_all, w_up, w_down):
    xg = np.concatenate([np.asarray(x) for x in xnt_all], 0)
    tbls = np.zeros((NCORES, 8 * ROWS_E), np.float32)
    islot = np.broadcast_to(np.arange(ROWS_E, dtype=np.float32), (8, ROWS_E)).copy()
    idn = np.eye(128, dtype=np.float32)
    cnt_mat = np.stack([np.asarray(c).reshape(-1) for c in cnt_all])
    in_maps = []
    for c in range(NCORES):
        for r in range(NCORES):
            tbls_r = np.asarray(tbl_all[r]).reshape(EMOE, ROWS_E)
            tbls[r] = tbls_r[8 * c:8 * c + 8].reshape(-1)
        cntT = np.ascontiguousarray(cnt_mat[:, 8 * c:8 * c + 8].T)
        in_maps.append({
            "xg": xg,
            "tbls": tbls.copy(),
            "cntT": _f32(cntT),
            "wup": _f32(w_up[8 * c:8 * c + 8].reshape(8 * D, 512)),
            "wdn": _f32(w_down[8 * c:8 * c + 8].reshape(8 * 256, D)),
            "islot": islot,
            "idn": idn,
        })
    return in_maps



def build_moe_combine():
    nc = bass.Bass("TRN2", target_bir_lowering=False)
    yall = nc.dram_tensor("yall", [EMOE * ROWS_E, D], BF16, kind="ExternalInput").ap()
    slot = nc.dram_tensor("slot", [TC, TOPK], F32, kind="ExternalInput").ap()
    wv = nc.dram_tensor("wv", [TC, TOPK], F32, kind="ExternalInput").ap()
    sht = nc.dram_tensor("sht", [TC, D], BF16, kind="ExternalInput").ap()
    x1t = nc.dram_tensor("x1t", [TC, D], F32, kind="ExternalInput").ap()
    x2_o = nc.dram_tensor("x2t", [TC, D], F32, kind="ExternalOutput").ap()

    NT = 16
    with contextlib.ExitStack() as ctx:
        sslot = ctx.enter_context(nc.sbuf_tensor("sslot", [128, TOPK], F32))
        sslo32 = ctx.enter_context(nc.sbuf_tensor("sslo32", [128, TOPK], I32))
        swv = ctx.enter_context(nc.sbuf_tensor("swv", [128, TOPK], F32))
        sga = ctx.enter_context(nc.sbuf_tensor("sga", [128, 2 * D], BF16))
        ssh = ctx.enter_context(nc.sbuf_tensor("ssh", [128, D], BF16))
        sx1 = ctx.enter_context(nc.sbuf_tensor("sx1", [128, D], F32))
        sacc = ctx.enter_context(nc.sbuf_tensor("sacc", [128, 2 * D], F32))

        block = ctx.enter_context(nc.Block())

        class NS:
            pass
        E = NS()

        @block.vector
        def _(v):
            E.ve = Eng(ctx, nc, v, "ve")

        @block.gpsimd
        def _(g):
            E.gq = Dma(ctx, nc, g, "gq")

        @block.sync
        def _(s):
            E.sq = Dma(ctx, nc, s, "sq")

        ve, gq, sq = E.ve, E.gq, E.sq
        lds = Sem(ctx, nc, "lds")
        gat = [Sem(ctx, nc, "gat0"), Sem(ctx, nc, "gat1")]
        stx = [Sem(ctx, nc, "stx0"), Sem(ctx, nc, "stx1")]

        K = {}
        for tt in range(NT):
            par = tt % 2

            @block.sync
            def _(s, tt=tt, par=par):
                if tt >= 1:
                    sq.wait(ve, K[("acc", tt - 1)])
                sq.start(sslot[:, :], slot[tt * 128:(tt + 1) * 128, :], sem=lds)
                sq.start(swv[:, :], wv[tt * 128:(tt + 1) * 128, :], sem=lds)
                sq.start(ssh[:, :], sht[tt * 128:(tt + 1) * 128, :], sem=lds)
                sq.start(sx1[:, :], x1t[tt * 128:(tt + 1) * 128, :], sem=lds)
            K[("lds", tt)] = lds.n

            @block.vector
            def _(v, tt=tt, par=par):
                ve.wait(lds, K[("lds", tt)])
                if tt >= 1:
                    ve.wait(gat[(tt - 1) % 2], K[("ga", tt - 1, TOPK - 1)])
                if tt >= 2:
                    ve.wait(stx[par], 16 * (tt // 2))
                ve.tensor_copy(sslo32[:, :], sslot[:, :])
                ve.tensor_add(sacc[:, par * D:(par + 1) * D], sx1[:, :], ssh[:, :])
            K[("i32", tt)] = ve.n

            for k in range(TOPK):
                @block.gpsimd
                def _(g, tt=tt, k=k, par=par):
                    if k == 0:
                        gq.wait(ve, K[("i32", tt)])
                    else:
                        gq.wait(ve, K[("k", tt, k - 1)])
                    gq.indirect(out=sga[:, par * D:(par + 1) * D], out_offset=None,
                                in_=yall,
                                in_offset=IndirectOffsetOnAxis(sslo32[:, k:k + 1], axis=0),
                                sem=gat[par])
                K[("ga", tt, k)] = gat[par].n

                @block.vector
                def _(v, tt=tt, k=k, par=par):
                    ve.wait(gat[par], K[("ga", tt, k)])
                    ve.scalar_tensor_tensor(sacc[:, par * D:(par + 1) * D],
                                            sga[:, par * D:(par + 1) * D],
                                            swv[:, k:k + 1],
                                            sacc[:, par * D:(par + 1) * D],
                                            ALU.mult, ALU.add)
                K[("k", tt, k)] = ve.n
            K[("acc", tt)] = ve.n

            @block.sync
            def _(s, tt=tt, par=par):
                sq.wait(ve, K[("acc", tt)])
                sq.start(x2_o[tt * 128:(tt + 1) * 128, :], sacc[:, par * D:(par + 1) * D],
                         sem=stx[par])

        @block.sync
        def _(s):
            for sem in (stx[0], stx[1], lds):
                s.wait_ge(sem.h, sem.n)

        @block.gpsimd
        def _(g):
            for sem in (gat[0], gat[1]):
                g.wait_ge(sem.h, sem.n)

    return nc


def host_moe_combine(y_all_cores, slot_all, wv_all, sht_all, x1_all):
    yall = np.concatenate([np.asarray(y) for y in y_all_cores], 0)
    in_maps = []
    for c in range(NCORES):
        in_maps.append({
            "yall": yall,
            "slot": _f32(slot_all[c]),
            "wv": _f32(wv_all[c]),
            "sht": sht_all[c] if np.asarray(sht_all[c]).dtype == BF else _bf(sht_all[c]),
            "x1t": _f32(x1_all[c]),
        })
    return in_maps
N_HEAD = 16
N_KV = 4
HD = 128
ROPE_EPS = 1e-6


def build_qkv():
    nc = bass.Bass("TRN2", target_bir_lowering=False)
    x2 = nc.dram_tensor("x2", [D, TC], F32, kind="ExternalInput").ap()
    wq = nc.dram_tensor("wq", [D, D], F32, kind="ExternalInput").ap()
    wk = nc.dram_tensor("wk", [D, 512], F32, kind="ExternalInput").ap()
    wv = nc.dram_tensor("wv", [D, 512], F32, kind="ExternalInput").ap()
    cosr = nc.dram_tensor("cosr", [128, TC], F32, kind="ExternalInput").ap()
    sinr = nc.dram_tensor("sinr", [128, TC], F32, kind="ExternalInput").ap()
    ones_bf = nc.dram_tensor("ones_bf", [128, 128], BF16, kind="ExternalInput").ap()

    qh_o = nc.dram_tensor("qh", [D, TC], BF16, kind="ExternalOutput").ap()
    kh_o = nc.dram_tensor("kh", [512, TC], BF16, kind="ExternalOutput").ap()
    v_o = nc.dram_tensor("vv", [512, TC], BF16, kind="ExternalOutput").ap()
    xn_o = nc.dram_tensor("xn2", [D, TC], BF16, kind="ExternalOutput").ap()

    NG = 4
    GWD = 512
    with contextlib.ExitStack() as ctx:
        sx = ctx.enter_context(nc.sbuf_tensor("sx", [128, 16 * GWD], F32))
        sxn = ctx.enter_context(nc.sbuf_tensor("sxn", [128, 16 * TC], BF16))
        sx2b = ctx.enter_context(nc.sbuf_tensor("sx2b", [128, 2 * GWD], BF16))
        sm = ctx.enter_context(nc.sbuf_tensor("sm", [128, GWD], F32))
        srs = ctx.enter_context(nc.sbuf_tensor("srs", [128, GWD], F32))
        svar = ctx.enter_context(nc.sbuf_tensor("svar", [128, GWD], F32))
        sonesb = ctx.enter_context(nc.sbuf_tensor("sonesb", [128, 128], BF16))
        sonef = ctx.enter_context(nc.sbuf_tensor("sonef", [128, 128], F32))
        sw = ctx.enter_context(nc.sbuf_tensor("sw", [128, 2 * 16 * 128], BF16))
        shead = ctx.enter_context(nc.sbuf_tensor("shead", [128, 2 * TC], BF16))
        sqo = ctx.enter_context(nc.sbuf_tensor("sqo", [128, TC], BF16))
        scos = ctx.enter_context(nc.sbuf_tensor("scos", [128, TC], F32))
        ssin = ctx.enter_context(nc.sbuf_tensor("ssin", [128, TC], F32))
        st1 = ctx.enter_context(nc.sbuf_tensor("st1", [128, TC], F32))
        st2 = ctx.enter_context(nc.sbuf_tensor("st2", [128, TC], F32))
        sq2 = ctx.enter_context(nc.sbuf_tensor("sq2", [128, TC], BF16))
        srms = ctx.enter_context(nc.sbuf_tensor("srms", [128, TC], F32))

        pS1 = ctx.enter_context(nc.psum_tensor("pS1", [128, 512], F32))
        pS2 = ctx.enter_context(nc.psum_tensor("pS2", [128, 512], F32))
        pmm = ctx.enter_context(nc.psum_tensor("pmm", [128, 2 * 512], F32))
        pn = ctx.enter_context(nc.psum_tensor("pn", [128, 2 * 512], F32))

        block = ctx.enter_context(nc.Block())

        class NS:
            pass
        E = NS()

        @block.vector
        def _(v):
            E.ve = Eng(ctx, nc, v, "ve")

        @block.scalar
        def _(s):
            E.se = Eng(ctx, nc, s, "se")

        @block.tensor
        def _(t):
            E.pe = Pe(ctx, nc)

        @block.gpsimd
        def _(g):
            E.gq = Dma(ctx, nc, g, "gq")

        @block.sync
        def _(s):
            E.sq = Dma(ctx, nc, s, "sq")

        ve, se, pe, gq, sq = E.ve, E.se, E.pe, E.gq, E.sq
        ldc = Sem(ctx, nc, "ldc")
        ldx = Sem(ctx, nc, "ldx")
        ldw = [Sem(ctx, nc, "ldw0"), Sem(ctx, nc, "ldw1")]
        sto = Sem(ctx, nc, "sto")

        @block.sync
        def _(s):
            sq.start(sonesb[:, :], ones_bf, sem=ldc)
            sq.start(scos[:, :], cosr, sem=ldc)
            sq.start(ssin[:, :], sinr, sem=ldc)
        ldc_done = ldc.n

        K = {}
        for g in range(NG):
            go = g * GWD

            @block.gpsimd
            def _(gg, g=g, go=go):
                if g >= 1:
                    gq.wait(ve, K[("xn", g - 1)])
                for k in range(16):
                    gq.start(sx[:, k * GWD:(k + 1) * GWD],
                             x2[k * 128:(k + 1) * 128, go:go + GWD], sem=ldx)
            K[("ldx", g)] = ldx.n

            for k in range(16):
                pp = k % 2

                @block.vector
                def _(v, g=g, k=k, pp=pp):
                    if k == 0:
                        ve.wait(ldx, K[("ldx", g)])
                        if g == 0:
                            ve.wait(ldc, ldc_done)
                            ve.memset(sonef[:, :], 1.0)
                    if k >= 2:
                        ve.wait(pe, K[("st2", g, k - 2)])
                    ve.tensor_mul(sx2b[:, pp * GWD:(pp + 1) * GWD],
                                  sx[:, k * GWD:(k + 1) * GWD],
                                  sx[:, k * GWD:(k + 1) * GWD])
                    K[("x2", g, k)] = ve.n

                @block.tensor
                def _(t, g=g, k=k, pp=pp):
                    if g > 0 and k == 0:
                        pe.wait(se, K[("rs", g - 1)])
                    pe.wait(ve, K[("x2", g, k)])
                    pe.matmul(pS1[:, :], sonef[:], sx[:, k * GWD:(k + 1) * GWD],
                              start=(k == 0), stop=(k == 15))
                    i = pe.matmul(pS2[:, :], sonesb[:], sx2b[:, pp * GWD:(pp + 1) * GWD],
                                  start=(k == 0), stop=(k == 15))
                    K[("st2", g, k)] = pe.done(i)

            @block.vector
            def _(v, g=g):
                ve.wait(pe, K[("st2", g, 15)])
                ve.tensor_scalar_mul(sm[:, :], pS1[:, :], 1.0 / D)
                ve.tensor_mul(svar[:, :], sm[:, :], sm[:, :])
                ve.scalar_tensor_tensor(svar[:, :], pS2[:, :], 1.0 / D, svar[:, :],
                                        ALU.mult, ALU.subtract)
                ve.tensor_scalar_add(svar[:, :], svar[:, :], LN_EPS)
            K[("var", g)] = ve.n

            @block.scalar
            def _(s, g=g):
                se.wait(ve, K[("var", g)])
                se.activation(srs[:, 0:GWD], svar[:, :], AF.Ln)
                se.activation(srs[:, 0:GWD], srs[:, 0:GWD], AF.Exp, scale=-0.5)
            K[("rs", g)] = se.n

            @block.vector
            def _(v, g=g, go=go):
                ve.wait(se, K[("rs", g)])
                for k in range(16):
                    ve.tensor_sub(sx[:, k * GWD:(k + 1) * GWD], sx[:, k * GWD:(k + 1) * GWD],
                                  sm[:, :])
                    ve.tensor_mul(sxn[:, k * TC + go:k * TC + go + GWD],
                                  sx[:, k * GWD:(k + 1) * GWD], srs[:, 0:GWD])
            K[("xn", g)] = ve.n

        @block.sync
        def _(s):
            sq.wait(ve, K[("xn", NG - 1)])
            for k in range(16):
                sq.start(xn_o[k * 128:(k + 1) * 128, :], sxn[:, k * TC:(k + 1) * TC], sem=sto)

        for ot in range(24):
            pp = ot % 2
            if ot < 16:
                wsrc, off = wq, ot * 128
                outdst = qh_o[ot * 128:(ot + 1) * 128, :]
            elif ot < 20:
                wsrc, off = wk, (ot - 16) * 128
                outdst = kh_o[(ot - 16) * 128:(ot - 15) * 128, :]
            else:
                wsrc, off = wv, (ot - 20) * 128
                outdst = v_o[(ot - 20) * 128:(ot - 19) * 128, :]

            @block.gpsimd
            def _(g, ot=ot, pp=pp, wsrc=wsrc, off=off):
                if ot >= 2:
                    gq.wait(pe, K[("mm", ot - 2, NG - 1)])
                gq.start(sw[:, pp * 2048:(pp + 1) * 2048],
                         wsrc[:, off:off + 128].rearrange("(k p) e -> p k e", p=128),
                         sem=ldw[pp])
            K[("ldw", ot)] = 16 * (ot // 2 + 1)

            for g in range(NG):
                idx = ot * NG + g
                bank = idx % 2

                @block.tensor
                def _(t, ot=ot, pp=pp, g=g, idx=idx, bank=bank):
                    if g == 0:
                        pe.wait(ldw[pp], K[("ldw", ot)])
                        if ot == 0:
                            pe.wait(ve, K[("xn", NG - 1)])
                    if idx >= 2:
                        pidx = idx - 2
                        pe.wait(se, K[("ev", pidx // NG, pidx % NG)])
                    for k in range(16):
                        i = pe.matmul(pmm[:, bank * 512:(bank + 1) * 512],
                                      sw[:, pp * 2048 + k * 128:pp * 2048 + (k + 1) * 128],
                                      sxn[:, k * TC + g * GWD:k * TC + (g + 1) * GWD],
                                      start=(k == 0), stop=(k == 15))
                    K[("mm", ot, g)] = pe.done(i)

                @block.scalar
                def _(s, ot=ot, pp=pp, g=g, idx=idx, bank=bank):
                    if g == 0:
                        if ot >= 2 and ot - 2 < 20:
                            se.wait(ve, K[("rmul", ot - 2, NG - 1)])
                        elif ot >= 2:
                            se.wait(sto, sto.n)
                    se.wait(pe, K[("mm", ot, g)])
                    se.copy(shead[:, pp * TC + g * GWD:pp * TC + (g + 1) * GWD],
                            pmm[:, bank * 512:(bank + 1) * 512])
                    K[("ev", ot, g)] = se.n

            if ot >= 20:
                @block.sync
                def _(s, ot=ot, pp=pp, outdst=outdst):
                    sq.wait(se, K[("ev", ot, NG - 1)])
                    sq.start(outdst, shead[:, pp * TC:(pp + 1) * TC], sem=sto)
            else:
                @block.vector
                def _(v, ot=ot, pp=pp):
                    ve.wait(se, K[("ev", ot, NG - 1)])
                    if ot >= 1:
                        ve.wait(sto, sto.n)
                    x1 = shead[:, pp * TC:(pp + 1) * TC]
                    ve.tensor_copy(st2[0:64, 0:TC], x1[64:128, :])
                    ve.tensor_copy(st2[64:128, 0:TC], x1[0:64, :])
                    ve.tensor_tensor(st1[0:64, 0:TC], x1[0:64, :], scos[0:64, :], ALU.mult)
                    ve.tensor_tensor(st1[64:128, 0:TC], x1[64:128, :], scos[64:128, :], ALU.mult)
                    ve.tensor_tensor(st2[0:64, 0:TC], st2[0:64, 0:TC], ssin[0:64, :], ALU.mult)
                    ve.tensor_tensor(st2[64:128, 0:TC], st2[64:128, 0:TC], ssin[64:128, :], ALU.mult)
                    ve.tensor_add(st1[0:64, 0:TC], st1[0:64, 0:TC], st2[0:64, 0:TC])
                    ve.tensor_sub(st1[64:128, 0:TC], st1[64:128, 0:TC], st2[64:128, 0:TC])
                    ve.tensor_mul(st2[:, 0:TC], st1[:, 0:TC], st1[:, 0:TC])
                    ve.tensor_copy(sq2[:, 0:TC], st2[:, 0:TC])
                K[("rope", ot)] = ve.n

                for g in range(NG):
                    @block.tensor
                    def _(t, ot=ot, g=g):
                        pe.wait(ve, K[("rope", ot)])
                        prev = K.get(("rv", ot, g - 2) if g >= 2 else ("rv", ot - 1, g + 2))
                        if prev:
                            pe.wait(ve, prev)
                        i = pe.matmul(pn[:, (g % 2) * 512:(g % 2) * 512 + 512], sonesb[:, :],
                                      sq2[:, g * 512:(g + 1) * 512], start=True, stop=True)
                        K[("rn", ot, g)] = pe.done(i)

                    @block.vector
                    def _(v, ot=ot, g=g):
                        ve.wait(pe, K[("rn", ot, g)])
                        ve.tensor_scalar(srms[:, g * 512:(g + 1) * 512],
                                         pn[:, (g % 2) * 512:(g % 2) * 512 + 512],
                                         1.0 / HD, ROPE_EPS, ALU.mult, ALU.add)
                    K[("rv", ot, g)] = ve.n

                    @block.scalar
                    def _(s, ot=ot, g=g):
                        se.wait(ve, K[("rv", ot, g)])
                        se.activation(srms[:, g * 512:(g + 1) * 512],
                                      srms[:, g * 512:(g + 1) * 512], AF.Ln)
                        se.activation(srms[:, g * 512:(g + 1) * 512],
                                      srms[:, g * 512:(g + 1) * 512], AF.Exp, scale=-0.5)
                    K[("srm", ot, g)] = se.n

                    @block.vector
                    def _(v, ot=ot, g=g):
                        ve.wait(se, K[("srm", ot, g)])
                        ve.tensor_mul(sqo[:, g * GWD:(g + 1) * GWD],
                                      st1[:, g * GWD:(g + 1) * GWD],
                                      srms[:, g * 512:(g + 1) * 512])
                    K[("rmul", ot, g)] = ve.n

                @block.sync
                def _(s, ot=ot, outdst=outdst):
                    for g in range(NG):
                        sq.wait(ve, K[("rmul", ot, g)])
                    sq.start(outdst, sqo[:, :], sem=sto)

        @block.sync
        def _(s):
            for sem in (sto, ldc):
                s.wait_ge(sem.h, sem.n)

        @block.gpsimd
        def _(g):
            for sem in (ldx, ldw[0], ldw[1]):
                g.wait_ge(sem.h, sem.n)

    return nc


def host_qkv(x2_all, wq, wk, wv):
    ones_bf = np.ones((128, 128), dtype=BF)
    half = HD // 2
    inv_freq = 1.0 / (100000.0 ** (np.arange(half, dtype=np.float32) / half))
    in_maps = []
    for c in range(NCORES):
        s0 = c * TC
        t0 = s0 % T
        pos = np.arange(t0, t0 + TC, dtype=np.float32)
        fr = pos[None, :] * inv_freq[:, None]
        in_maps.append({
            "x2": _f32(np.asarray(x2_all[c % len(x2_all)]).T),
            "wq": _f32(wq), "wk": _f32(wk), "wv": _f32(wv),
            "cosr": _f32(np.concatenate([np.cos(fr), np.cos(fr)], 0)),
            "sinr": _f32(np.concatenate([np.sin(fr), np.sin(fr)], 0)),
            "ones_bf": ones_bf,
        })
    return in_maps



def build_attn():
    nc = bass.Bass("TRN2", target_bir_lowering=False)
    NU = 4
    qh = nc.dram_tensor("qh", [NU * 128, T], BF16, kind="ExternalInput").ap()
    kh = nc.dram_tensor("kh", [NU * 128, T], BF16, kind="ExternalInput").ap()
    vp = nc.dram_tensor("vp", [NU * T, 128], BF16, kind="ExternalInput").ap()
    dmask = nc.dram_tensor("dmask", [128, 4 * 512], F32, kind="ExternalInput").ap()
    ones_bf = nc.dram_tensor("ones_bf", [128, 128], BF16, kind="ExternalInput").ap()
    y_out = nc.dram_tensor("yat", [NU * 128, T], BF16, kind="ExternalOutput").ap()

    NQT = T // 512
    SCALE = 1.0 / np.sqrt(128.0)

    with contextlib.ExitStack() as ctx:
        sq_ = ctx.enter_context(nc.sbuf_tensor("sq_", [128, 2 * T], BF16))
        sk = ctx.enter_context(nc.sbuf_tensor("sk", [128, 2 * T], BF16))
        sv = ctx.enter_context(nc.sbuf_tensor("sv", [128, 2 * T], BF16))
        sp = ctx.enter_context(nc.sbuf_tensor("sp", [128, 2 * 512], BF16))
        sms = ctx.enter_context(nc.sbuf_tensor("sms", [128, 2 * 512], F32))
        smk = ctx.enter_context(nc.sbuf_tensor("smk", [128, 4 * 512], F32))
        sonesb = ctx.enter_context(nc.sbuf_tensor("sonesb", [128, 128], BF16))
        sli = ctx.enter_context(nc.sbuf_tensor("sli", [128, 512], F32))
        sy = ctx.enter_context(nc.sbuf_tensor("sy", [128, 2 * 512], BF16))

        pS = ctx.enter_context(nc.psum_tensor("pS", [128, 2 * 512], F32))
        pL = ctx.enter_context(nc.psum_tensor("pL", [128, 512], F32))
        pY = ctx.enter_context(nc.psum_tensor("pY", [128, 512], F32))

        block = ctx.enter_context(nc.Block())

        class NS:
            pass
        E = NS()

        @block.vector
        def _(v):
            E.ve = Eng(ctx, nc, v, "ve")

        @block.scalar
        def _(s):
            E.se = Eng(ctx, nc, s, "se")

        @block.tensor
        def _(t):
            E.pe = Pe(ctx, nc)

        @block.gpsimd
        def _(g):
            E.gq = Dma(ctx, nc, g, "gq")

        @block.sync
        def _(s):
            E.sq = Dma(ctx, nc, s, "sq")

        ve, se, pe, gq, sq = E.ve, E.se, E.pe, E.gq, E.sq
        ldc = Sem(ctx, nc, "ldc")
        ldq = [Sem(ctx, nc, "ldq0"), Sem(ctx, nc, "ldq1")]
        ldk = [Sem(ctx, nc, "ldk0"), Sem(ctx, nc, "ldk1")]
        ldv = [Sem(ctx, nc, "ldv0"), Sem(ctx, nc, "ldv1")]
        sty = [Sem(ctx, nc, "sty0"), Sem(ctx, nc, "sty1")]

        @block.sync
        def _(s):
            sq.start(sonesb[:, :], ones_bf, sem=ldc)
            sq.start(smk[:, :], dmask, sem=ldc)
        ldc_done = ldc.n

        K = {}
        for u in range(4):
            upp = u % 2

            @block.gpsimd
            def _(g, u=u, upp=upp):
                if u >= 2:
                    gq.wait(pe, K[("ulast", u - 2)])
                gq.start(sq_[:, upp * T:(upp + 1) * T], qh[u * 128:(u + 1) * 128, :],
                         sem=ldq[upp])
                gq.start(sk[:, upp * T:(upp + 1) * T], kh[u * 128:(u + 1) * 128, :],
                         sem=ldk[upp])
                gq.start(sv[:, upp * T:(upp + 1) * T],
                         vp[u * T:(u + 1) * T, :].rearrange("(n p) d -> p n d", p=128),
                         sem=ldv[upp])
            K[("ld", u)] = (16 * (u // 2 + 1))

            for qt in range(NQT):
                nkt = 4 * qt + 4
                for kt in range(nkt):
                    it = (u, qt, kt)
                    par = kt % 2
                    diag = (kt >= 4 * qt)

                    @block.tensor
                    def _(t, u=u, upp=upp, qt=qt, kt=kt, par=par):
                        if (u, qt, kt) == (0, 0, 0):
                            pe.wait(ldc, ldc_done)
                        if qt == 0 and kt == 0:
                            pe.wait(ldq[upp], K[("ld", u)])
                            pe.wait(ldk[upp], K[("ld", u)])
                            pe.wait(ldv[upp], K[("ld", u)])
                        prev = K.get(("exp", u, qt, kt - 2)) or \
                            (K.get(("exp", u, qt - 1, 4 * qt - 1)) if kt - 2 < 0 and qt > 0 else None) or \
                            (K.get(("exp", u - 1, NQT - 1, 4 * NQT - 1)) if kt - 2 < 0 and qt == 0 and u > 0 else None)
                        if kt >= 2:
                            pe.wait(se, K[("exp", u, qt, kt - 2)])
                        elif (qt > 0 or u > 0):
                            pk = (u, qt - 1, 4 * (qt - 1) + 3) if qt > 0 else (u - 1, NQT - 1, 4 * (NQT - 1) + 3)
                            if kt == 0:
                                pe.wait(se, K[("exp", pk[0], pk[1], pk[2] - 1)])
                            else:
                                pe.wait(se, K[("exp", pk[0], pk[1], pk[2])])
                        i = pe.matmul(pS[:, par * 512:(par + 1) * 512],
                                      sk[:, upp * T + kt * 128:upp * T + (kt + 1) * 128],
                                      sq_[:, upp * T + qt * 512:upp * T + (qt + 1) * 512],
                                      start=True, stop=True)
                        K[("smm", u, qt, kt)] = pe.done(i)

                    if diag:
                        @block.vector
                        def _(v, u=u, qt=qt, kt=kt, par=par):
                            ve.wait(pe, K[("smm", u, qt, kt)])
                            mi = kt - 4 * qt
                            ve.tensor_tensor(sms[:, par * 512:(par + 1) * 512],
                                             pS[:, par * 512:(par + 1) * 512],
                                             smk[:, mi * 512:(mi + 1) * 512], ALU.min)
                        K[("min", u, qt, kt)] = ve.n

                    @block.scalar
                    def _(s, u=u, qt=qt, kt=kt, par=par, diag=diag):
                        if diag:
                            se.wait(ve, K[("min", u, qt, kt)])
                            src = sms[:, par * 512:(par + 1) * 512]
                        else:
                            se.wait(pe, K[("smm", u, qt, kt)])
                            src = pS[:, par * 512:(par + 1) * 512]
                        if kt >= 2:
                            se.wait(pe, K[("av", u, qt, kt - 2)])
                        se.activation(sp[:, par * 512:(par + 1) * 512], src, AF.Exp,
                                      scale=SCALE)
                    K[("exp", u, qt, kt)] = se.n

                    @block.tensor
                    def _(t, u=u, upp=upp, qt=qt, kt=kt, par=par):
                        pe.wait(se, K[("exp", u, qt, kt)])
                        if kt == 0 and (qt > 0 or u > 0):
                            pu, pqt = (u, qt - 1) if qt > 0 else (u - 1, NQT - 1)
                            pe.wait(ve, K[("ep", pu, pqt)])
                        i = pe.matmul(pL[:, :], sonesb[:, :], sp[:, par * 512:(par + 1) * 512],
                                      start=(kt == 0), stop=(kt == 4 * qt + 3))
                        i = pe.matmul(pY[:, :], sv[:, upp * T + kt * 128:upp * T + (kt + 1) * 128],
                                      sp[:, par * 512:(par + 1) * 512],
                                      start=(kt == 0), stop=(kt == 4 * qt + 3))
                        K[("av", u, qt, kt)] = pe.done(i)

                @block.vector
                def _(v, u=u, qt=qt):
                    ve.wait(pe, K[("av", u, qt, 4 * qt + 3)])
                    gi = u * NQT + qt
                    if gi >= 2:
                        ve.wait(sty[gi % 2], 16 * (gi // 2))
                    ve.reciprocal(sli[:, :], pL[:, :])
                    ve.tensor_tensor(sy[:, (gi % 2) * 512:(gi % 2 + 1) * 512],
                                     pY[:, :], sli[:, :], ALU.mult)
                K[("ep", u, qt)] = ve.n

                @block.sync
                def _(s, u=u, qt=qt):
                    sq.wait(ve, K[("ep", u, qt)])
                    gi = u * NQT + qt
                    sq.start(y_out[u * 128:(u + 1) * 128, qt * 512:(qt + 1) * 512],
                             sy[:, (gi % 2) * 512:(gi % 2 + 1) * 512], sem=sty[gi % 2])
            K[("ulast", u)] = pe.n

        @block.sync
        def _(s):
            for sem in (sty[0], sty[1], ldc):
                s.wait_ge(sem.h, sem.n)

        @block.gpsimd
        def _(g):
            for sem in (ldq[0], ldq[1], ldk[0], ldk[1], ldv[0], ldv[1]):
                g.wait_ge(sem.h, sem.n)

    return nc


def host_attn(qh_all, kh_all, v_all):
    qh_full = np.zeros((2, N_HEAD * 128, T), dtype=BF)
    kh_full = np.zeros((2, N_KV * 128, T), dtype=BF)
    v_full = np.zeros((2, N_KV * 128, T), dtype=BF)
    for c in range(NCORES):
        s0 = c * TC
        b0, t0 = s0 // T, s0 % T
        qh_full[b0, :, t0:t0 + TC] = np.asarray(qh_all[c])
        kh_full[b0, :, t0:t0 + TC] = np.asarray(kh_all[c])
        v_full[b0, :, t0:t0 + TC] = np.asarray(v_all[c])
    MASKV = -10000.0 * np.sqrt(128.0)
    dmask = np.zeros((128, 4 * 512), np.float32)
    for mi in range(4):
        q_idx = np.arange(512)[None, :]
        k_idx = (np.arange(128) + mi * 128)[:, None]
        dmask[:, mi * 512:(mi + 1) * 512] = np.where(q_idx >= k_idx, 3e38, MASKV)
    ones_bf = np.ones((128, 128), dtype=BF)
    in_maps = []
    for c in range(NCORES):
        qs, ks, vs, = [], [], []
        for un in range(4):
            uid = c * 4 + un
            b, h = uid // 16, uid % 16
            kvh = h // 4
            qs.append(qh_full[b, h * 128:(h + 1) * 128, :])
            ks.append(kh_full[b, kvh * 128:(kvh + 1) * 128, :])
            vs.append(np.ascontiguousarray(v_full[b, kvh * 128:(kvh + 1) * 128, :].T))
        in_maps.append({
            "qh": np.concatenate(qs, 0), "kh": np.concatenate(ks, 0),
            "vp": np.concatenate(vs, 0),
            "dmask": dmask, "ones_bf": ones_bf,
        })
    return in_maps



def build_attnout():
    nc = bass.Bass("TRN2", target_bir_lowering=False)
    ya = nc.dram_tensor("ya", [D, TC], BF16, kind="ExternalInput").ap()
    wo = nc.dram_tensor("wo", [D, D], F32, kind="ExternalInput").ap()
    x2 = nc.dram_tensor("x2", [D, TC], F32, kind="ExternalInput").ap()
    x3_o = nc.dram_tensor("x3", [D, TC], F32, kind="ExternalOutput").ap()

    NG = 4
    GWD = 512
    with contextlib.ExitStack() as ctx:
        sy = ctx.enter_context(nc.sbuf_tensor("sy", [128, 16 * TC], BF16))
        sw = ctx.enter_context(nc.sbuf_tensor("sw", [128, 2 * 2048], BF16))
        sr = ctx.enter_context(nc.sbuf_tensor("sr", [128, 2 * GWD], F32))
        so = ctx.enter_context(nc.sbuf_tensor("so", [128, 2 * GWD], F32))
        pmm = ctx.enter_context(nc.psum_tensor("pmm", [128, 2 * 512], F32))

        block = ctx.enter_context(nc.Block())

        class NS:
            pass
        E = NS()

        @block.vector
        def _(v):
            E.ve = Eng(ctx, nc, v, "ve")

        @block.tensor
        def _(t):
            E.pe = Pe(ctx, nc)

        @block.gpsimd
        def _(g):
            E.gq = Dma(ctx, nc, g, "gq")

        @block.sync
        def _(s):
            E.sq = Dma(ctx, nc, s, "sq")

        ve, pe, gq, sq = E.ve, E.pe, E.gq, E.sq
        ldy = Sem(ctx, nc, "ldy")
        ldw = [Sem(ctx, nc, "ldw0"), Sem(ctx, nc, "ldw1")]
        ldr = [Sem(ctx, nc, "ldr0"), Sem(ctx, nc, "ldr1")]
        stx = [Sem(ctx, nc, "stx0"), Sem(ctx, nc, "stx1")]

        @block.sync
        def _(s):
            for k in range(16):
                sq.start(sy[:, k * TC:(k + 1) * TC], ya[k * 128:(k + 1) * 128, :], sem=ldy)
        ldy_done = ldy.n

        K = {}
        for gi in range(64):
            ot, grp = gi // 4, gi % 4
            pp = ot % 2
            spp = gi % 2
            go = grp * GWD

            if grp == 0:
                @block.gpsimd
                def _(g, ot=ot, pp=pp):
                    if ot >= 2:
                        gq.wait(pe, K[("mm", (ot - 1) * 4 + 3)])
                    gq.start(sw[:, pp * 2048:(pp + 1) * 2048],
                             wo[:, ot * 128:(ot + 1) * 128].rearrange("(k p) e -> p k e", p=128),
                             sem=ldw[pp])

            @block.sync
            def _(s, gi=gi, ot=ot, go=go, spp=spp):
                if gi >= 2:
                    sq.wait(ve, K[("o", gi - 2)])
                sq.start(sr[:, spp * GWD:(spp + 1) * GWD],
                         x2[ot * 128:(ot + 1) * 128, go:go + GWD], sem=ldr[spp])
            K[("ldr", gi)] = ldr[spp].n

            @block.tensor
            def _(t, gi=gi, ot=ot, grp=grp, pp=pp, spp=spp, go=go):
                if grp == 0:
                    pe.wait(ldw[pp], 16 * (ot // 2 + 1))
                    if ot == 0:
                        pe.wait(ldy, ldy_done)
                if gi >= 2:
                    pe.wait(ve, K[("o", gi - 2)])
                for k in range(16):
                    i = pe.matmul(pmm[:, spp * 512:(spp + 1) * 512],
                                  sw[:, pp * 2048 + k * 128:pp * 2048 + (k + 1) * 128],
                                  sy[:, k * TC + go:k * TC + go + GWD],
                                  start=(k == 0), stop=(k == 15))
                K[("mm", gi)] = pe.done(i)

            @block.vector
            def _(v, gi=gi, spp=spp):
                ve.wait(pe, K[("mm", gi)])
                ve.wait(ldr[spp], K[("ldr", gi)])
                if gi >= 2:
                    ve.wait(stx[spp], 16 * (gi // 2))
                ve.tensor_add(so[:, spp * GWD:(spp + 1) * GWD],
                              pmm[:, spp * 512:(spp + 1) * 512],
                              sr[:, spp * GWD:(spp + 1) * GWD])
            K[("o", gi)] = ve.n

            @block.sync
            def _(s, gi=gi, ot=ot, go=go, spp=spp):
                sq.wait(ve, K[("o", gi)])
                sq.start(x3_o[ot * 128:(ot + 1) * 128, go:go + GWD],
                         so[:, spp * GWD:(spp + 1) * GWD], sem=stx[spp])

        @block.sync
        def _(s):
            for sem in (stx[0], stx[1], ldy, ldr[0], ldr[1]):
                s.wait_ge(sem.h, sem.n)

        @block.gpsimd
        def _(g):
            for sem in (ldw[0], ldw[1]):
                g.wait_ge(sem.h, sem.n)

    return nc


def host_attnout(yat_all, wo, x2_all):
    y_full = np.zeros((2, N_HEAD * 128, T), dtype=BF)
    for c in range(NCORES):
        for un in range(4):
            uid = c * 4 + un
            b, h = uid // 16, uid % 16
            y_full[b, h * 128:(h + 1) * 128, :] = np.asarray(yat_all[c])[un * 128:(un + 1) * 128, :]
    in_maps = []
    for c in range(NCORES):
        s0 = c * TC
        b0, t0 = s0 // T, s0 % T
        in_maps.append({
            "ya": np.ascontiguousarray(y_full[b0, :, t0:t0 + TC]),
            "wo": _f32(wo),
            "x2": _f32(x2_all[c]),
        })
    return in_maps


INTER = 5632
NIT = 44


def build_ffn():
    nc = bass.Bass("TRN2", target_bir_lowering=False)
    x3 = nc.dram_tensor("x3", [D, TC], F32, kind="ExternalInput").ap()
    gwT = nc.dram_tensor("gwT", [D, INTER], F32, kind="ExternalInput").ap()
    uwT = nc.dram_tensor("uwT", [D, INTER], F32, kind="ExternalInput").ap()
    dwT = nc.dram_tensor("dwT", [INTER, D], F32, kind="ExternalInput").ap()
    gs = nc.dram_tensor("gs", [128, NIT], F32, kind="ExternalInput").ap()
    us = nc.dram_tensor("us", [128, NIT], F32, kind="ExternalInput").ap()
    ds = nc.dram_tensor("ds", [128, 16], F32, kind="ExternalInput").ap()
    ones_bf = nc.dram_tensor("ones_bf", [128, 128], BF16, kind="ExternalInput").ap()
    x4_o = nc.dram_tensor("x4", [D, TC], F32, kind="ExternalOutput").ap()
    h_d = nc.dram_tensor("h_d", [INTER, TC], BF16).ap()

    NG = 4
    GWD = 512
    with contextlib.ExitStack() as ctx:
        sx = ctx.enter_context(nc.sbuf_tensor("sx", [128, 16 * GWD], F32))
        sxn = ctx.enter_context(nc.sbuf_tensor("sxn", [128, 16 * TC], BF16))
        sx2b = ctx.enter_context(nc.sbuf_tensor("sx2b", [128, 2 * GWD], BF16))
        sm = ctx.enter_context(nc.sbuf_tensor("sm", [128, GWD], F32))
        srs = ctx.enter_context(nc.sbuf_tensor("srs", [128, GWD], F32))
        svar = ctx.enter_context(nc.sbuf_tensor("svar", [128, GWD], F32))
        sonesb = ctx.enter_context(nc.sbuf_tensor("sonesb", [128, 128], BF16))
        sonef = ctx.enter_context(nc.sbuf_tensor("sonef", [128, 128], F32))
        sgw = ctx.enter_context(nc.sbuf_tensor("sgw", [128, 2 * 2048], BF16))
        suw = ctx.enter_context(nc.sbuf_tensor("suw", [128, 2 * 2048], BF16))
        sdw = ctx.enter_context(nc.sbuf_tensor("sdw", [128, 2 * NIT * 128], BF16))
        sgs = ctx.enter_context(nc.sbuf_tensor("sgs", [128, NIT], F32))
        sus = ctx.enter_context(nc.sbuf_tensor("sus", [128, NIT], F32))
        sds = ctx.enter_context(nc.sbuf_tensor("sds", [128, 16], F32))
        sg = ctx.enter_context(nc.sbuf_tensor("sg", [128, TC], BF16))
        sh = ctx.enter_context(nc.sbuf_tensor("sh", [128, TC], BF16))
        shl = ctx.enter_context(nc.sbuf_tensor("shl", [128, NIT * GWD], BF16))
        sr = ctx.enter_context(nc.sbuf_tensor("sr", [128, 2 * GWD], F32))
        so = ctx.enter_context(nc.sbuf_tensor("so", [128, 2 * GWD], F32))

        pS1 = ctx.enter_context(nc.psum_tensor("pS1", [128, 512], F32))
        pS2 = ctx.enter_context(nc.psum_tensor("pS2", [128, 512], F32))
        pg = ctx.enter_context(nc.psum_tensor("pg", [128, 2 * 512], F32))
        pu = ctx.enter_context(nc.psum_tensor("pu", [128, 2 * 512], F32))
        pd = ctx.enter_context(nc.psum_tensor("pd", [128, 2 * 512], F32))

        block = ctx.enter_context(nc.Block())

        class NS:
            pass
        E = NS()

        @block.vector
        def _(v):
            E.ve = Eng(ctx, nc, v, "ve")

        @block.scalar
        def _(s):
            E.se = Eng(ctx, nc, s, "se")

        @block.tensor
        def _(t):
            E.pe = Pe(ctx, nc)

        @block.gpsimd
        def _(g):
            E.gq = Dma(ctx, nc, g, "gq")

        @block.sync
        def _(s):
            E.sq = Dma(ctx, nc, s, "sq")

        ve, se, pe, gq, sq = E.ve, E.se, E.pe, E.gq, E.sq
        ldc = Sem(ctx, nc, "ldc")
        ldx = Sem(ctx, nc, "ldx")
        ldg = [Sem(ctx, nc, "ldg0"), Sem(ctx, nc, "ldg1")]
        ldu = [Sem(ctx, nc, "ldu0"), Sem(ctx, nc, "ldu1")]
        ldd = [Sem(ctx, nc, "ldd0"), Sem(ctx, nc, "ldd1")]
        ldh = [Sem(ctx, nc, "ldh0"), Sem(ctx, nc, "ldh1")]
        ldr = [Sem(ctx, nc, "ldr0"), Sem(ctx, nc, "ldr1")]
        sth = [Sem(ctx, nc, "sth0"), Sem(ctx, nc, "sth1")]
        stx = [Sem(ctx, nc, "stx0"), Sem(ctx, nc, "stx1")]

        @block.sync
        def _(s):
            sq.start(sonesb[:, :], ones_bf, sem=ldc)
            sq.start(sgs[:, :], gs, sem=ldc)
            sq.start(sus[:, :], us, sem=ldc)
            sq.start(sds[:, :], ds, sem=ldc)
        ldc_done = ldc.n

        K = {}
        for g in range(NG):
            go = g * GWD

            @block.gpsimd
            def _(gg, g=g, go=go):
                if g >= 1:
                    gq.wait(ve, K[("xn", g - 1)])
                for k in range(16):
                    gq.start(sx[:, k * GWD:(k + 1) * GWD],
                             x3[k * 128:(k + 1) * 128, go:go + GWD], sem=ldx)
            K[("ldx", g)] = ldx.n

            for k in range(16):
                pp = k % 2

                @block.vector
                def _(v, g=g, k=k, pp=pp):
                    if k == 0:
                        ve.wait(ldx, K[("ldx", g)])
                        if g == 0:
                            ve.wait(ldc, ldc_done)
                            ve.memset(sonef[:, :], 1.0)
                    if k >= 2:
                        ve.wait(pe, K[("st2", g, k - 2)])
                    ve.tensor_mul(sx2b[:, pp * GWD:(pp + 1) * GWD],
                                  sx[:, k * GWD:(k + 1) * GWD],
                                  sx[:, k * GWD:(k + 1) * GWD])
                    K[("x2", g, k)] = ve.n

                @block.tensor
                def _(t, g=g, k=k, pp=pp):
                    if g > 0 and k == 0:
                        pe.wait(se, K[("rs", g - 1)])
                    pe.wait(ve, K[("x2", g, k)])
                    pe.matmul(pS1[:, :], sonef[:], sx[:, k * GWD:(k + 1) * GWD],
                              start=(k == 0), stop=(k == 15))
                    i = pe.matmul(pS2[:, :], sonesb[:], sx2b[:, pp * GWD:(pp + 1) * GWD],
                                  start=(k == 0), stop=(k == 15))
                    K[("st2", g, k)] = pe.done(i)

            @block.vector
            def _(v, g=g):
                ve.wait(pe, K[("st2", g, 15)])
                ve.tensor_scalar_mul(sm[:, :], pS1[:, :], 1.0 / D)
                ve.tensor_mul(svar[:, :], sm[:, :], sm[:, :])
                ve.scalar_tensor_tensor(svar[:, :], pS2[:, :], 1.0 / D, svar[:, :],
                                        ALU.mult, ALU.subtract)
                ve.tensor_scalar_add(svar[:, :], svar[:, :], LN_EPS)
            K[("var", g)] = ve.n

            @block.scalar
            def _(s, g=g):
                se.wait(ve, K[("var", g)])
                se.activation(srs[:, 0:GWD], svar[:, :], AF.Ln)
                se.activation(srs[:, 0:GWD], srs[:, 0:GWD], AF.Exp, scale=-0.5)
            K[("rs", g)] = se.n

            @block.vector
            def _(v, g=g, go=go):
                ve.wait(se, K[("rs", g)])
                for k in range(16):
                    ve.tensor_sub(sx[:, k * GWD:(k + 1) * GWD], sx[:, k * GWD:(k + 1) * GWD],
                                  sm[:, :])
                    ve.tensor_mul(sxn[:, k * TC + go:k * TC + go + GWD],
                                  sx[:, k * GWD:(k + 1) * GWD], srs[:, 0:GWD])
            K[("xn", g)] = ve.n

        for it in range(NIT):
            pp = it % 2

            @block.gpsimd
            def _(g, it=it, pp=pp):
                if it >= 2:
                    gq.wait(pe, K[("gum", it - 2)])
                gq.start(sgw[:, pp * 2048:(pp + 1) * 2048],
                         gwT[:, it * 128:(it + 1) * 128].rearrange("(k p) e -> p k e", p=128),
                         sem=ldg[pp])
                gq.start(suw[:, pp * 2048:(pp + 1) * 2048],
                         uwT[:, it * 128:(it + 1) * 128].rearrange("(k p) e -> p k e", p=128),
                         sem=ldu[pp])

            for grp in range(NG):
                @block.tensor
                def _(t, it=it, pp=pp, grp=grp):
                    if grp == 0:
                        pe.wait(ldg[pp], 16 * (it // 2 + 1))
                        pe.wait(ldu[pp], 16 * (it // 2 + 1))
                        if it == 0:
                            pe.wait(ve, K[("xn", NG - 1)])
                    prev = K.get(("h", it, grp - 2) if grp >= 2 else ((it - 1, grp + 2)
                                 and ("h", it - 1, grp + 2)))
                    if prev:
                        pe.wait(ve, prev)
                    for k in range(16):
                        pe.matmul(pg[:, (grp % 2) * 512:(grp % 2 + 1) * 512],
                                  sgw[:, pp * 2048 + k * 128:pp * 2048 + (k + 1) * 128],
                                  sxn[:, k * TC + grp * GWD:k * TC + (grp + 1) * GWD],
                                  start=(k == 0), stop=(k == 15))
                        i = pe.matmul(pu[:, (grp % 2) * 512:(grp % 2 + 1) * 512],
                                      suw[:, pp * 2048 + k * 128:pp * 2048 + (k + 1) * 128],
                                      sxn[:, k * TC + grp * GWD:k * TC + (grp + 1) * GWD],
                                      start=(k == 0), stop=(k == 15))
                    K[("gum", it, grp)] = pe.done(i)
                    if grp == NG - 1:
                        K[("gum", it)] = pe.n

                @block.scalar
                def _(s, it=it, grp=grp):
                    se.wait(pe, K[("gum", it, grp)])
                    se.activation(sg[:, grp * GWD:(grp + 1) * GWD],
                                  pg[:, (grp % 2) * 512:(grp % 2 + 1) * 512], AF.Silu,
                                  scale=sgs[:, it:it + 1])
                    K[("sg", it, grp)] = se.n

                @block.vector
                def _(v, it=it, grp=grp):
                    ve.wait(se, K[("sg", it, grp)])
                    if grp == 0 and it >= 1:
                        ve.wait(sth[0], K[("sth", it - 1)])
                    ve.scalar_tensor_tensor(sh[:, grp * GWD:(grp + 1) * GWD],
                                            pu[:, (grp % 2) * 512:(grp % 2 + 1) * 512],
                                            sus[:, it:it + 1],
                                            sg[:, grp * GWD:(grp + 1) * GWD],
                                            ALU.mult, ALU.mult)
                K[("h", it, grp)] = ve.n

            @block.sync
            def _(s, it=it):
                sq.wait(ve, K[("h", it, NG - 1)])
                sq.start(h_d[it * 128:(it + 1) * 128, :], sh[:, 0:TC], sem=sth[0])
            K[("sth", it)] = sth[0].n

        for grp in range(NG):
            go = grp * GWD

            @block.sync
            def _(s, grp=grp, go=go):
                if grp == 0:
                    s.wait_ge(sth[0].h, sth[0].n)
                else:
                    sq.wait(pe, K[("dmm", grp - 1, 15)])
                for it in range(NIT):
                    sq.start(shl[:, it * GWD:(it + 1) * GWD],
                             h_d[it * 128:(it + 1) * 128, go:go + GWD], sem=ldh[grp % 2])
            K[("ldh", grp)] = ldh[grp % 2].n

            for ot in range(16):
                gi = grp * 16 + ot
                pp = gi % 2

                @block.gpsimd
                def _(g, grp=grp, ot=ot, gi=gi, pp=pp):
                    if gi >= 2:
                        gq.wait(pe, K[("dmm", (gi - 2) // 16, (gi - 2) % 16)])
                    gq.start(sdw[:, pp * NIT * 128:(pp + 1) * NIT * 128],
                             dwT[:, ot * 128:(ot + 1) * 128]
                             .rearrange("(k p) e -> p k e", p=128),
                             sem=ldd[pp])

                @block.sync
                def _(s, grp=grp, go=go, ot=ot, gi=gi, pp=pp):
                    if gi >= 2:
                        sq.wait(ve, K[("o", gi - 2)])
                    sq.start(sr[:, (gi % 2) * GWD:(gi % 2 + 1) * GWD],
                             x3[ot * 128:(ot + 1) * 128, go:go + GWD], sem=ldr[gi % 2])
                K[("ldr", gi)] = ldr[gi % 2].n

                @block.tensor
                def _(t, grp=grp, ot=ot, gi=gi, pp=pp):
                    pe.wait(ldd[pp], 16 * (gi // 2 + 1))
                    if ot == 0:
                        pe.wait(ldh[grp % 2], K[("ldh", grp)])
                    if gi >= 2:
                        pe.wait(ve, K[("o", gi - 2)])
                    for k in range(NIT):
                        i = pe.matmul(pd[:, (gi % 2) * 512:(gi % 2 + 1) * 512],
                                      sdw[:, pp * NIT * 128 + k * 128:pp * NIT * 128 + (k + 1) * 128],
                                      shl[:, k * GWD:(k + 1) * GWD],
                                      start=(k == 0), stop=(k == NIT - 1))
                    K[("dmm", grp, ot)] = pe.done(i)

                @block.vector
                def _(v, grp=grp, ot=ot, gi=gi):
                    ve.wait(pe, K[("dmm", grp, ot)])
                    ve.wait(ldr[gi % 2], K[("ldr", gi)])
                    if gi >= 2:
                        ve.wait(stx[gi % 2], 16 * (gi // 2))
                    ve.scalar_tensor_tensor(so[:, (gi % 2) * GWD:(gi % 2 + 1) * GWD],
                                            pd[:, (gi % 2) * 512:(gi % 2 + 1) * 512],
                                            sds[:, ot:ot + 1],
                                            sr[:, (gi % 2) * GWD:(gi % 2 + 1) * GWD],
                                            ALU.mult, ALU.add)
                K[("o", gi)] = ve.n

                @block.sync
                def _(s, grp=grp, go=go, ot=ot, gi=gi):
                    sq.wait(ve, K[("o", gi)])
                    sq.start(x4_o[ot * 128:(ot + 1) * 128, go:go + GWD],
                             so[:, (gi % 2) * GWD:(gi % 2 + 1) * GWD], sem=stx[gi % 2])

        @block.sync
        def _(s):
            for sem in (stx[0], stx[1], sth[0], ldc, ldr[0], ldr[1], ldh[0], ldh[1]):
                s.wait_ge(sem.h, sem.n)

        @block.gpsimd
        def _(g):
            for sem in (ldx, ldg[0], ldg[1], ldu[0], ldu[1], ldd[0], ldd[1]):
                g.wait_ge(sem.h, sem.n)

    return nc


def host_ffn(x3_all, gate_w, up_w, down_w, gate_s, up_s, down_s):
    gwT = _f32(gate_w.T)
    uwT = _f32(up_w.T)
    dwT = _f32(down_w.T)
    gs_t = _f32(gate_s.reshape(NIT, 128).T)
    us_t = _f32(up_s.reshape(NIT, 128).T)
    ds_t = _f32(down_s.reshape(16, 128).T)
    ones_bf = np.ones((128, 128), dtype=BF)
    in_maps = []
    for c in range(NCORES):
        in_maps.append({
            "x3": _f32(x3_all[c]),
            "gwT": gwT, "uwT": uwT, "dwT": dwT,
            "gs": gs_t, "us": us_t, "ds": ds_t,
            "ones_bf": ones_bf,
        })
    return in_maps



_NC_CACHE = {}


def _get(name, builder):
    if name not in _NC_CACHE:
        _NC_CACHE[name] = builder()
    return _NC_CACHE[name]


def kernel(**inputs):
    I = {k: np.asarray(v) for k, v in inputs.items()}
    u_flat = _f32(I["L0_attn_u"].reshape(NTOK, D))
    R = list(range(NCORES))

    r1 = run_bass_kernel_spmd(_get("l1", build_mamba_front),
                              host_mamba_front(u_flat, I["L0_attn_in_proj_w"],
                                               I["L0_attn_conv_w"], I["L0_attn_conv_b"],
                                               I["L0_attn_dt_bias"], I["L0_attn_A_log"]),
                              core_ids=R).results
    r2 = run_bass_kernel_spmd(_get("l2", build_ssd),
                              host_ssd([r1[c]["xc"] for c in R], [r1[c]["Bf"] for c in R],
                                       [r1[c]["Cf"] for c in R], [r1[c]["dtv"] for c in R],
                                       [r1[c]["cumA"] for c in R], I["L0_attn_D"]),
                              core_ids=R).results
    r3 = run_bass_kernel_spmd(_get("l3", build_mgate),
                              host_mgate([r2[c]["y_pm"] for c in R], [r1[c]["zs"] for c in R],
                                         u_flat, I["L0_attn_out_proj_w"], I["L0_attn_norm_w"]),
                              core_ids=R).results
    r4a = run_bass_kernel_spmd(_get("l4a", build_moe_route),
                               host_moe_route([r3[c]["x1_fm"] for c in R], I["L0_ffn_w_gate"],
                                              I["L0_ffn_w_shared_up"], I["L0_ffn_w_shared_down"]),
                               core_ids=R).results
    r4b = run_bass_kernel_spmd(_get("l4b", build_moe_expert),
                               host_moe_expert([r4a[c]["xnt"] for c in R],
                                               [r4a[c]["tbl"] for c in R],
                                               [r4a[c]["cnt"] for c in R],
                                               I["L0_ffn_w_up"], I["L0_ffn_w_down"]),
                               core_ids=R).results
    r4c = run_bass_kernel_spmd(_get("l4c", build_moe_combine),
                               host_moe_combine([r4b[c]["Yout"] for c in R],
                                                [r4a[c]["slot"] for c in R],
                                                [r4a[c]["wv"] for c in R],
                                                [r4a[c]["sht"] for c in R],
                                                [np.asarray(r3[c]["x1_fm"]).T for c in R]),
                               core_ids=R).results
    r5 = run_bass_kernel_spmd(_get("l5", build_qkv),
                              host_qkv([r4c[c]["x2t"] for c in R], I["L1_attn_wq"],
                                       I["L1_attn_wk"], I["L1_attn_wv"]),
                              core_ids=R).results
    r6 = run_bass_kernel_spmd(_get("l6", build_attn),
                              host_attn([r5[c]["qh"] for c in R], [r5[c]["kh"] for c in R],
                                        [r5[c]["vv"] for c in R]),
                              core_ids=R).results
    r7 = run_bass_kernel_spmd(_get("l7", build_attnout),
                              host_attnout([r6[c]["yat"] for c in R], I["L1_attn_wo"],
                                           [np.asarray(r4c[c]["x2t"]).T for c in R]),
                              core_ids=R).results
    r8 = run_bass_kernel_spmd(_get("l8", build_ffn),
                              host_ffn([r7[c]["x3"] for c in R], I["L1_ffn_gate_w"],
                                       I["L1_ffn_up_w"], I["L1_ffn_down_w"],
                                       I["L1_ffn_gate_s"], I["L1_ffn_up_s"],
                                       I["L1_ffn_down_s"]),
                              core_ids=R).results
    out = np.concatenate([np.asarray(r8[c]["x4"], np.float32).T for c in R], 0)
    return out.reshape(B, T, D)
